# version 12
# baseline (speedup 1.0000x reference)
import numpy as np
import ml_dtypes
import concourse.bass as bass
import concourse.mybir as mybir
from concourse.bass_utils import run_bass_kernel_spmd
from contextlib import ExitStack

F32 = mybir.dt.float32
BF16 = mybir.dt.bfloat16
AF = mybir.ActivationFunctionType
OP = mybir.AluOpType
AX = mybir.AxisListType

N = 2048
E = 2048
H = 16
D = 128
NEXP = 8
MLP = 4096
NT = 16
NG = 4
GS = 512
HALF = 1024
NEG = -1.0e30
ENGINES = ("tensor", "vector", "scalar", "gpsimd", "sync")
PREP_HEAD = 64
SA_SCALE = 2.18
WEAVE_LOG = []


class Op:
    __slots__ = ("engine", "fn", "deps", "dma", "ndma", "lane", "ev", "needed", "tag")

    def __init__(self, engine, fn, deps, dma, ndma, tag):
        self.engine = engine
        self.fn = fn
        self.deps = [d for d in deps if d is not None]
        self.dma = dma
        self.ndma = ndma
        self.lane = None
        self.ev = None
        self.needed = False
        self.tag = tag


class Prog:

    NDMA_SEMS = 16

    def __init__(self, nc, ctx):
        self.nc = nc
        self.ops = []
        self.sems = {}
        self.sem_vals = {}
        self.dma_idx = {}
        for e in ENGINES:
            name = f"s_{e}_c"
            self.sems[name] = ctx.enter_context(nc.semaphore(name))
            self.sem_vals[name] = 0
            self.dma_idx[e] = 0
            for i in range(self.NDMA_SEMS):
                name = f"s_{e}_d{i}"
                self.sems[name] = ctx.enter_context(nc.semaphore(name))
                self.sem_vals[name] = 0
        self.last = {e: None for e in ENGINES}
        self.last_compute = {e: None for e in ENGINES}
        self.last_xpose = None
        self.lane_last = {e: [None] * self.NDMA_SEMS for e in ENGINES}

    def op(self, engine, fn, deps=(), dma=False, ndma=1, xpose=False, tag=""):
        o = Op(engine, fn, list(deps), dma, ndma, tag)
        if xpose and self.last_xpose is not None:
            o.deps.append(self.last_xpose)
        if engine in ("vector", "scalar") and not dma \
                and self.last_compute[engine] is not None:
            o.deps.append(self.last_compute[engine])
        if dma:
            o.lane = self.dma_idx[engine]
            self.dma_idx[engine] = (o.lane + 1) % self.NDMA_SEMS
            self.lane_last[engine][o.lane] = o
        else:
            self.last_compute[engine] = o
        if xpose:
            self.last_xpose = o
        self.ops.append(o)
        self.last[engine] = o
        return o

    def fence(self, engine):
        out = [o for o in self.lane_last[engine] if o is not None]
        if self.last_compute[engine] is not None:
            out.append(self.last_compute[engine])
        if self.last[engine] is not None and self.last[engine] not in out:
            out.append(self.last[engine])
        return out

    def fence_all(self):
        return [o for e in ENGINES for o in self.fence(e)]

    def emit_phase(self, block, start_idx):
        ops = self.ops[start_idx:]
        cur = set(id(o) for o in ops)
        for o in ops:
            for d in o.deps:
                if id(d) not in cur and d.ev is None:
                    raise RuntimeError(
                        f"dep {d.tag} of {o.tag} is from an already-emitted phase "
                        f"but was not pinned")
                d.needed = True
        for o in ops:
            if (o.needed or o.dma) and o.ev is None:
                if o.dma:
                    name = f"s_{o.engine}_d{o.lane}"
                    step = 16 * o.ndma
                else:
                    name = f"s_{o.engine}_c"
                    step = 1
                self.sem_vals[name] += step
                o.ev = (name, self.sem_vals[name])
        per_engine = {e: [] for e in ENGINES}
        for o in ops:
            per_engine[o.engine].append(o)
        for e in ENGINES:
            recs = per_engine[e]
            if not recs:
                continue
            sec = getattr(block, e)

            def section(eng, recs=recs):
                seen = {}
                for o in recs:
                    for d in o.deps:
                        if d.ev is None:
                            raise RuntimeError(f"dep of {o.tag} has no event: {d.tag}")
                        s, v = d.ev
                        if seen.get(s, 0) < v:
                            eng.wait_ge(self.sems[s], v)
                            seen[s] = v
                    if o.dma and o.ev is not None:
                        s, v = o.ev
                        prev = v - 16 * o.ndma
                        if prev > 0 and seen.get(s, 0) < prev:
                            eng.wait_ge(self.sems[s], prev)
                            seen[s] = prev
                    inst = o.fn(eng)
                    if o.ev is not None:
                        insts = inst if isinstance(inst, list) else [inst]
                        assert len(insts) == (o.ndma if o.dma else 1), o.tag
                        for i2 in insts:
                            i2.then_inc(self.sems[o.ev[0]], 16 if o.dma else 1)

            sec(section)
        return len(self.ops)


class Region:

    def __init__(self):
        self.w = None
        self.readers = []

    def wdeps(self):
        return ([self.w] if self.w is not None else []) + self.readers

    def on_write(self, op):
        self.w = op
        self.readers = []

    def rdeps(self):
        return [self.w] if self.w is not None else []

    def on_read(self, op):
        self.readers.append(op)


def rw(regions_read, regions_written, extra=()):
    deps = list(extra)
    for r in regions_read:
        deps += r.rdeps()
    for r in regions_written:
        deps += r.wdeps()
    return deps


def commit(op, regions_read, regions_written):
    for r in regions_read:
        r.on_read(op)
    for r in regions_written:
        r.on_write(op)
    return op


def weave(streams):
    clocks = [0.0] * len(streams)
    idx = [0] * len(streams)
    while True:
        cands = [i for i in range(len(streams)) if idx[i] < len(streams[i])]
        if not cands:
            break
        i = min(cands, key=lambda k: clocks[k])
        est, fn = streams[i][idx[i]]
        idx[i] += 1
        fn()
        clocks[i] += est
        WEAVE_LOG.append((i, est))


def build_program():
    nc = bass.Bass("TRN2", target_bir_lowering=False)

    x_d = nc.dram_tensor("x", [N, E], F32, kind="ExternalInput").ap()
    wq_d = nc.dram_tensor("Wq", [E, E], F32, kind="ExternalInput").ap()
    wkv_d = nc.dram_tensor("Wkv", [256, E], F32, kind="ExternalInput").ap()
    wkvh_d = nc.dram_tensor("Wkvh", [32, E], F32, kind="ExternalInput").ap()
    wg_d = nc.dram_tensor("Wg", [32, E], F32, kind="ExternalInput").ap()
    wout_d = nc.dram_tensor("Wout", [E, E], F32, kind="ExternalInput").ap()
    gate_d = nc.dram_tensor("gate", [E, NEXP], F32, kind="ExternalInput").ap()
    w0_d = nc.dram_tensor("w0", [NEXP, E, MLP], F32, kind="ExternalInput").ap()
    w1_d = nc.dram_tensor("w1", [NEXP, E, MLP], F32, kind="ExternalInput").ap()
    wo_d = nc.dram_tensor("wo", [NEXP, MLP, E], F32, kind="ExternalInput").ap()
    costk_d = nc.dram_tensor("cos_tok", [N, 64], BF16, kind="ExternalInput").ap()
    sintk_d = nc.dram_tensor("sin_tok", [N, 64], BF16, kind="ExternalInput").ap()
    cosT_d = nc.dram_tensor("cosT", [64, N], BF16, kind="ExternalInput").ap()
    sinT_d = nc.dram_tensor("sinT", [64, N], BF16, kind="ExternalInput").ap()
    mask4_d = nc.dram_tensor("mask4", [128, 2048], BF16, kind="ExternalInput").ap()
    maskT4_d = nc.dram_tensor("maskT4", [128, 2048], BF16, kind="ExternalInput").ap()
    d16_d = nc.dram_tensor("delta16", [16, 16 * 128], BF16, kind="ExternalInput").ap()
    d8_d = nc.dram_tensor("delta8", [8, 8 * 128], BF16, kind="ExternalInput").ap()
    ones1_d = nc.dram_tensor("ones1", [1, 128], F32, kind="ExternalInput").ap()
    ident_d = nc.dram_tensor("ident", [128, 128], F32, kind="ExternalInput").ap()
    y_d = nc.dram_tensor("y", [N, E], F32, kind="ExternalOutput").ap()

    xnT_s = nc.dram_tensor("xnT_s", [16, 128, N], BF16, kind="Internal").ap()
    wqT_s = nc.dram_tensor("wqT_s", [16, 128, E], BF16, kind="Internal").ap()
    woutT_s = nc.dram_tensor("woutT_s", [16, 128, E], BF16, kind="Internal").ap()
    w0b_s = nc.dram_tensor("w0b_s", [NEXP, 16, 128, 4096], BF16, kind="Internal").ap()
    w1b_s = nc.dram_tensor("w1b_s", [NEXP, 16, 128, 4096], BF16, kind="Internal").ap()
    wob_s = nc.dram_tensor("wob_s", [NEXP, 16, 128, 4096], BF16, kind="Internal").ap()
    qT_s = nc.dram_tensor("qT_s", [16, 128, N], BF16, kind="Internal").ap()
    oT_s = nc.dram_tensor("oT_s", [16, 128, N], BF16, kind="Internal").ap()
    attn_s = nc.dram_tensor("attn_s", [N, E], F32, kind="Internal").ap()
    ffnT_s = nc.dram_tensor("ffnT_s", [E, N], F32, kind="Internal").ap()

    with ExitStack() as root:
        def sbt(ctx, name, shape, dt):
            return ctx.enter_context(nc.sbuf_tensor(name, shape, dt))

        PSF = [root.enter_context(nc.psum_tensor(f"psf{b}", [128, 512], F32))
               for b in range(4)]
        PQ = root.enter_context(nc.psum_tensor("psq", [128, 512], F32))
        PS = root.enter_context(nc.psum_tensor("pss", [128, 1024], F32))
        PO = root.enter_context(nc.psum_tensor("pso", [128, 512], F32))
        BK = [Region() for _ in range(8)]

        def pslice(c0, c1):
            if c1 <= 2048:
                b = c0 // 512
                assert (c1 - 1) // 512 == b, (c0, c1)
                return PSF[b][:, c0 - 512 * b: c1 - 512 * b]
            if c1 <= 2560:
                assert c0 >= 2048
                return PQ[:, c0 - 2048: c1 - 2048]
            if c1 <= 3584:
                assert c0 >= 2560
                return PS[:, c0 - 2560: c1 - 2560]
            assert c0 >= 3584
            return PO[:, c0 - 3584: c1 - 3584]

        def banks(c0, c1):
            return [BK[b] for b in range(c0 // 512, (c1 + 511) // 512)]

        prog = Prog(nc, root)
        block = root.enter_context(nc.Block())

        d8_sb = sbt(root, "d8_sb", [8, 8 * 128], BF16)
        ones1_sb = sbt(root, "ones1_sb", [1, 128], F32)
        ident_sb = sbt(root, "ident_sb", [128, 128], F32)
        combT_sb = sbt(root, "combT_sb", [8, NT * 128], BF16)
        kpT_sb = sbt(root, "kpT_sb", [128, NT * 128], BF16)
        vp_sb = sbt(root, "vp_sb", [128, NT * 128], BF16)
        vhtok_sb = sbt(root, "vhtok_sb", [128, NT * 16], BF16)
        vhtok32_sb = sbt(root, "vhtok32_sb", [128, NT * 16], F32)
        khT_bf = sbt(root, "khT_bf", [16, N], BF16)
        pw0 = [sbt(root, f"pw0_{i}", [128, 1024], BF16) for i in range(2)]
        pw1 = [sbt(root, f"pw1_{i}", [128, 1024], BF16) for i in range(2)]
        pwo = [sbt(root, f"pwo_{i}", [128, 1024], BF16) for i in range(2)]

        qst = {}
        ct_done = []
        wout_st_by_h = {h: [] for h in range(16)}
        xn_store = None

        with ExitStack() as pscope:
            xnT_sb = sbt(pscope, "xnT_sb", [128, 16 * N], BF16)
            wgT_sb = sbt(pscope, "wgT_sb", [128, 16 * 32], BF16)
            wkvhT_sb = sbt(pscope, "wkvhT_sb", [128, 16 * 32], BF16)
            with ExitStack() as ps1:
                x_sb = [sbt(ps1, f"x_sb{i}", [128, E], F32) for i in range(2)]
                xntk_sb = [sbt(ps1, f"xntk{i}", [128, E], BF16) for i in range(2)]
                wld_sb = [sbt(ps1, f"wld{i}", [128, E], BF16) for i in range(2)]
                wstage_sb = [sbt(ps1, "wstage0", [128, E], BF16)]
                wkvT_sb = sbt(ps1, "wkvT_sb", [128, 16 * 256], BF16)
                gate_sb = sbt(ps1, "gate_sb", [128, 16 * 8], BF16)
                costk_sb = sbt(ps1, "costk_sb", [128, NT * 64], BF16)
                sintk_sb = sbt(ps1, "sintk_sb", [128, NT * 64], BF16)
                sumsq_sb = sbt(ps1, "sumsq_sb", [128, 16], F32)
                mean_sb = sbt(ps1, "mean_sb", [128, 1], F32)
                var_sb = sbt(ps1, "var_sb", [128, 1], F32)
                rstd_sb = sbt(ps1, "rstd_sb", [128, 1], F32)
                sq_sb = sbt(ps1, "sq_sb", [128, E], BF16)
                kprot_sb = [sbt(ps1, f"kprot{i}", [128, 128], BF16) for i in range(2)]
                ktmp = [sbt(ps1, f"ktmp{i}", [128, 64], F32) for i in range(2)]
                eq1_sb = sbt(ps1, "eq1_sb", [128, NT * 8], F32)
                eq2_sb = sbt(ps1, "eq2_sb", [128, NT * 8], F32)
                comb_sb = sbt(ps1, "comb_sb", [128, NT * 8], F32)
                dcol_sb = sbt(ps1, "dcol_sb", [128, 16], F32)
                w1c_sb = sbt(ps1, "w1c_sb", [128, 16], F32)
                w2c_sb = sbt(ps1, "w2c_sb", [128, 16], F32)
                m1_sb = sbt(ps1, "m1_sb", [128, 1], F32)
                m2_sb = sbt(ps1, "m2_sb", [128, 1], F32)

                RX = [Region() for _ in range(2)]
                RXN = [Region() for _ in range(2)]
                RWL = [Region() for _ in range(2)]
                RWS = [Region()]
                RKP = [Region() for _ in range(2)]
                Rstats = Region()

                const_ops = []

                def ld(dst, src):
                    o = prog.op("scalar", lambda e, d=dst, s=src: e.dma_start(out=d, in_=s),
                                dma=True, tag="cld")
                    const_ops.append(o)
                    return o
                ld(d8_sb[:], d8_d)
                ld(ones1_sb[:], ones1_d)
                ld(ident_sb[:], ident_d)
                ld(costk_sb[:].rearrange("p (c j) -> p c j", c=NT),
                   costk_d.rearrange("(c p) j -> p c j", p=128))
                ld(sintk_sb[:].rearrange("p (c j) -> p c j", c=NT),
                   sintk_d.rearrange("(c p) j -> p c j", p=128))

                for c in range(16):
                    lop = prog.op("gpsimd",
                                  lambda e, c=c, b=c % 2: e.dma_start(
                                      out=wld_sb[b][:], in_=wq_d[c * 128:(c + 1) * 128, :]),
                                  deps=rw([], [RWL[c % 2]]), dma=True, tag=f"wqld{c}")
                    commit(lop, [], [RWL[c % 2]])
                    top = prog.op("sync",
                                  lambda e, b=c % 2: e.dma_start_transpose(
                                      wstage_sb[0][:].rearrange("p (k d) -> p k d", k=16),
                                      wld_sb[b][:]),
                                  deps=rw([RWL[c % 2]], [RWS[0]]), dma=True, xpose=True,
                                  tag=f"wqtp{c}")
                    commit(top, [RWL[c % 2]], [RWS[0]])
                    sop = prog.op("scalar",
                                  lambda e, c=c: e.dma_start(
                                      out=wqT_s[c], in_=wstage_sb[0][:]),
                                  deps=rw([RWS[0]], []), dma=True, tag=f"wqst{c}")
                    commit(sop, [RWS[0]], [])
                for c in range(16):
                    lop = prog.op("gpsimd",
                                  lambda e, c=c, b=c % 2: e.dma_start(
                                      out=wld_sb[b][:], in_=wout_d[c * 128:(c + 1) * 128, :]),
                                  deps=rw([], [RWL[c % 2]]), dma=True, tag=f"wold{c}")
                    commit(lop, [], [RWL[c % 2]])
                    top = prog.op("sync",
                                  lambda e, b=c % 2: e.dma_start_transpose(
                                      wstage_sb[0][:].rearrange("p (k d) -> p k d", k=16),
                                      wld_sb[b][:]),
                                  deps=rw([RWL[c % 2]], [RWS[0]]), dma=True, xpose=True,
                                  tag=f"wotp{c}")
                    commit(top, [RWL[c % 2]], [RWS[0]])
                    for h in range(16):
                        sop = prog.op("scalar",
                                      lambda e, c=c, h=h: e.dma_start(
                                          out=woutT_s[h][:, c * 128:(c + 1) * 128],
                                          in_=wstage_sb[0][:, h * 128:(h + 1) * 128]),
                                      deps=rw([RWS[0]], []), dma=True, tag=f"wost{c}_{h}")
                        commit(sop, [RWS[0]], [])
                        wout_st_by_h[h].append(sop)

                kv_ld = []
                for c in range(2):
                    lop = prog.op("gpsimd",
                                  lambda e, c=c, b=c % 2: e.dma_start(
                                      out=wld_sb[b][:], in_=wkv_d[c * 128:(c + 1) * 128, :]),
                                  deps=rw([], [RWL[c % 2]]), dma=True, tag=f"kvld{c}")
                    commit(lop, [], [RWL[c % 2]])
                    top = prog.op("sync",
                                  lambda e, c=c, b=c % 2: e.dma_start_transpose(
                                      wkvT_sb[:].rearrange("p (k n) -> p k n", k=16)
                                      [:, :, c * 128:(c + 1) * 128],
                                      wld_sb[b][:]),
                                  deps=rw([RWL[c % 2]], []), dma=True, xpose=True,
                                  tag=f"kvtp{c}")
                    commit(top, [RWL[c % 2]], [])
                    kv_ld.append(top)
                g32 = prog.op("gpsimd", lambda e: e.dma_start(out=wld_sb[0][0:32, :], in_=wg_d),
                              deps=rw([], [RWL[0]]), dma=True, tag="g32")
                commit(g32, [], [RWL[0]])
                kvh32 = prog.op("gpsimd",
                                lambda e: e.dma_start(out=wld_sb[1][0:32, :], in_=wkvh_d),
                                deps=rw([], [RWL[1]]), dma=True, tag="kvh32")
                commit(kvh32, [], [RWL[1]])
                gtld = prog.op("gpsimd",
                               lambda e: e.dma_start(
                                   out=gate_sb[:].rearrange("p (c k) -> p c k", c=16),
                                   in_=gate_d.rearrange("(c p) k -> p c k", p=128)),
                               dma=True, tag="gtld")
                wgtp = prog.op("sync",
                               lambda e: e.dma_start_transpose(
                                   wgT_sb[:].rearrange("p (c k) -> p c k", c=16),
                                   wld_sb[0][0:32, :]),
                               deps=[g32], dma=True, xpose=True, tag="wgtp")
                commit(wgtp, [RWL[0]], [])
                kvhtp = prog.op("sync",
                                lambda e: e.dma_start_transpose(
                                    wkvhT_sb[:].rearrange("p (c k) -> p c k", c=16),
                                    wld_sb[1][0:32, :]),
                                deps=[kvh32], dma=True, xpose=True, tag="kvhtp")
                commit(kvhtp, [RWL[1]], [])

                xn_tp = []
                for t in range(NT):
                    lop = prog.op("scalar",
                                  lambda e, t=t, b=t % 2: e.dma_start(
                                      out=x_sb[b][:], in_=x_d[t * 128:(t + 1) * 128, :]),
                                  deps=rw([], [RX[t % 2]]), dma=True, tag=f"xld{t}")
                    commit(lop, [], [RX[t % 2]])
                    sqo = prog.op("scalar",
                                  lambda e, t=t, b=t % 2: e.activation(
                                      sq_sb[:], x_sb[b][:], AF.Square,
                                      accum_out=sumsq_sb[:, t:t + 1]),
                                  deps=rw([RX[t % 2]], []), tag=f"sq{t}")
                    commit(sqo, [RX[t % 2]], [])
                    vst = prog.op("vector",
                                  lambda e, b=t % 2: e.reduce_sum(mean_sb[:], x_sb[b][:], axis=AX.X),
                                  deps=rw([RX[t % 2]], [Rstats]), tag=f"vm{t}")
                    commit(vst, [RX[t % 2]], [Rstats])
                    prog.op("vector", lambda e: e.tensor_scalar(
                        mean_sb[:], mean_sb[:], 1.0 / E, None, OP.mult, OP.bypass), tag="vm2")
                    prog.op("vector", lambda e: e.tensor_tensor(
                        var_sb[:], mean_sb[:], mean_sb[:], OP.mult), tag="vm3")
                    prog.op("vector",
                            lambda e, t=t: e.scalar_tensor_tensor(
                                var_sb[:], sumsq_sb[:, t:t + 1], 1.0 / E, var_sb[:],
                                OP.mult, OP.subtract),
                            deps=[sqo], tag=f"vvar{t}")
                    vve = prog.op("vector", lambda e: e.tensor_scalar(
                        var_sb[:], var_sb[:], 1e-5, None, OP.add, OP.bypass), tag="veps")
                    prog.op("scalar", lambda e: e.activation(rstd_sb[:], var_sb[:], AF.Ln),
                            deps=[vve], tag=f"sln{t}")
                    sex = prog.op("scalar", lambda e: e.activation(
                        rstd_sb[:], rstd_sb[:], AF.Exp, scale=-0.5), tag=f"sex{t}")
                    vxn = prog.op("vector",
                                  lambda e, b=t % 2: e.tensor_scalar(
                                      xntk_sb[b][:], x_sb[b][:], mean_sb[:, 0:1],
                                      rstd_sb[:, 0:1], OP.subtract, OP.mult),
                                  deps=rw([RX[t % 2], Rstats], [RXN[t % 2]], extra=[sex]),
                                  tag=f"vxn{t}")
                    commit(vxn, [RX[t % 2], Rstats], [RXN[t % 2]])
                    commit(vxn, [], [RX[t % 2]])
                    top = prog.op("sync",
                                  lambda e, t=t, b=t % 2: e.dma_start_transpose(
                                      xnT_sb[:].rearrange("p (c n) -> p c n", c=16)
                                      [:, :, t * 128:(t + 1) * 128],
                                      xntk_sb[b][:]),
                                  deps=rw([RXN[t % 2]], []), dma=True, xpose=True,
                                  tag=f"xntp{t}")
                    commit(top, [RXN[t % 2]], [])
                    xn_tp.append(top)
                xn_tp_fence = prog.fence("sync")

                def xnT_store(e):
                    return [e.dma_start(
                        out=xnT_s[q * 8:(q + 1) * 8].rearrange("c p n -> p c n"),
                        in_=xnT_sb[:].rearrange("p (c n) -> p c n", c=16)
                        [:, q * 8:(q + 1) * 8, :]) for q in range(2)]
                xn_store = prog.op("scalar", xnT_store, deps=xn_tp_fence, dma=True, ndma=2,
                                   tag="xnst")

                dep_xnt = xn_tp_fence
                for t in range(NT):
                    def kvmm(e, t=t):
                        for ec in range(16):
                            inst = e.matmul(pslice(0, 256),
                                            xnT_sb[:, ec * N + t * 128: ec * N + t * 128 + 128],
                                            wkvT_sb[:, ec * 256:(ec + 1) * 256],
                                            start=(ec == 0), stop=(ec == 15))
                        return inst
                    mm = prog.op("tensor", kvmm,
                                 deps=rw([], [BK[0]], extra=dep_xnt + [kv_ld[1]] + const_ops),
                                 tag=f"kv{t}")
                    commit(mm, [], [BK[0]])
                    b = t % 2
                    prog.op("vector", lambda e, t=t: e.tensor_tensor(
                        ktmp[0][:], pslice(0, 64), costk_sb[:, t * 64:(t + 1) * 64], OP.mult),
                        deps=rw([BK[0]], [RKP[b]], extra=[mm]), tag=f"ropek{t}a")
                    prog.op("vector", lambda e, t=t: e.tensor_tensor(
                        ktmp[1][:], pslice(64, 128), sintk_sb[:, t * 64:(t + 1) * 64], OP.mult),
                        tag=f"ropek{t}b")
                    prog.op("vector", lambda e, b=b: e.tensor_tensor(
                        kprot_sb[b][:, 0:64], ktmp[0][:], ktmp[1][:], OP.subtract),
                        tag=f"ropek{t}c")
                    prog.op("vector", lambda e, t=t: e.tensor_tensor(
                        ktmp[0][:], pslice(64, 128), costk_sb[:, t * 64:(t + 1) * 64], OP.mult),
                        tag=f"ropek{t}d")
                    prog.op("vector", lambda e, t=t: e.tensor_tensor(
                        ktmp[1][:], pslice(0, 64), sintk_sb[:, t * 64:(t + 1) * 64], OP.mult),
                        tag=f"ropek{t}e")
                    rp = prog.op("vector", lambda e, b=b: e.tensor_tensor(
                        kprot_sb[b][:, 64:128], ktmp[0][:], ktmp[1][:], OP.add),
                        tag=f"ropek{t}f")
                    commit(rp, [BK[0]], [RKP[b]])
                    vpo = prog.op("vector",
                                  lambda e, t=t: e.tensor_copy(vp_sb[:, t * 128:(t + 1) * 128],
                                                               pslice(128, 256)),
                                  deps=[mm], tag=f"vp{t}")
                    commit(vpo, [BK[0]], [])
                    tp = prog.op("sync",
                                 lambda e, t=t, b=b: e.dma_start_transpose(
                                     kpT_sb[:, t * 128:(t + 1) * 128], kprot_sb[b][:]),
                                 deps=rw([RKP[b]], []), dma=True, xpose=True, tag=f"kptp{t}")
                    commit(tp, [RKP[b]], [])

                comb_parts = []
                for t in range(NT):
                    def gmm(e, t=t):
                        for ec in range(16):
                            inst = e.matmul(pslice(2560, 2568),
                                            xnT_sb[:, ec * N + t * 128: ec * N + t * 128 + 128],
                                            gate_sb[:, ec * 8:(ec + 1) * 8],
                                            start=(ec == 0), stop=(ec == 15))
                        return inst
                    mm = prog.op("tensor", gmm,
                                 deps=rw([], [BK[5]], extra=dep_xnt + [gtld]), tag=f"g8{t}")
                    commit(mm, [], [BK[5]])
                    prog.op("vector", lambda e: e.reduce_max(
                        m1_sb[:], pslice(2560, 2568), axis=AX.X), deps=[mm], tag=f"gv{t}a")
                    prog.op("vector", lambda e, t=t: e.tensor_scalar(
                        eq1_sb[:, t * 8:(t + 1) * 8], pslice(2560, 2568),
                        m1_sb[:, 0:1], None, OP.is_equal, OP.bypass), tag=f"gv{t}b")
                    prog.op("vector", lambda e, t=t: e.scalar_tensor_tensor(
                        comb_sb[:, t * 8:(t + 1) * 8], eq1_sb[:, t * 8:(t + 1) * 8], NEG,
                        pslice(2560, 2568), OP.mult, OP.add), tag=f"gv{t}c")
                    prog.op("vector", lambda e, t=t: e.reduce_max(
                        m2_sb[:], comb_sb[:, t * 8:(t + 1) * 8], axis=AX.X), tag=f"gv{t}d")
                    prog.op("vector", lambda e, t=t: e.tensor_scalar(
                        eq2_sb[:, t * 8:(t + 1) * 8], pslice(2560, 2568),
                        m2_sb[:, 0:1], None, OP.is_equal, OP.bypass), tag=f"gv{t}e")
                    gv = prog.op("vector", lambda e, t=t: e.tensor_tensor(
                        dcol_sb[:, t:t + 1], m1_sb[:], m2_sb[:], OP.subtract), tag=f"gv{t}f")
                    commit(gv, [BK[5]], [])
                    comb_parts.append(gv)
                sig = prog.op("scalar", lambda e: e.activation(w1c_sb[:], dcol_sb[:], AF.Sigmoid),
                              deps=[comb_parts[-1]], tag="sig")
                vw2 = prog.op("vector", lambda e: e.tensor_scalar(
                    w2c_sb[:], w1c_sb[:], -1.0, 1.0, OP.mult, OP.add), deps=[sig], tag="vw2")
                comb_done = []
                for t in range(NT):
                    prog.op("vector",
                            lambda e, t=t: e.tensor_scalar(
                                eq1_sb[:, t * 8:(t + 1) * 8], eq1_sb[:, t * 8:(t + 1) * 8],
                                w1c_sb[:, t:t + 1], None, OP.mult, OP.bypass),
                            deps=[vw2], tag=f"c1{t}")
                    cop = prog.op("vector",
                                  lambda e, t=t: e.scalar_tensor_tensor(
                                      comb_sb[:, t * 8:(t + 1) * 8], eq2_sb[:, t * 8:(t + 1) * 8],
                                      w2c_sb[:, t:t + 1], eq1_sb[:, t * 8:(t + 1) * 8],
                                      OP.mult, OP.add),
                                  tag=f"c2{t}")
                    comb_done.append(cop)
                for t in range(NT):
                    tp = prog.op("tensor",
                                 lambda e, t=t: e.matmul(PO[0:8, 0:128],
                                                         comb_sb[:, t * 8:(t + 1) * 8],
                                                         ident_sb[:], start=True, stop=True,
                                                         is_transpose=True),
                                 deps=rw([], [BK[7]], extra=[comb_done[t]] + const_ops),
                                 tag=f"ctT{t}")
                    commit(tp, [], [BK[7]])
                    cp = prog.op("vector",
                                 lambda e, t=t: e.tensor_copy(combT_sb[:, t * 128:(t + 1) * 128],
                                                              PO[0:8, 0:128]),
                                 deps=[tp], tag=f"ctC{t}")
                    commit(cp, [BK[7]], [])
                    ct_done.append(cp)

            p1f = prog.fence_all()
            with ExitStack() as ps2:
                G = {}
                for nm in ("pa", "pb", "lgs", "Fx", "tmpa", "tmpb"):
                    G[nm] = sbt(ps2, "G_" + nm, [16, N], F32)
                vhT_bf = sbt(ps2, "vhT_bf", [16, N], BF16)
                cosT_sb = sbt(ps2, "cosT_sb", [64, N], BF16)
                sinT_sb = sbt(ps2, "sinT_sb", [64, N], BF16)
                rtmp_sb = [sbt(ps2, f"rtmp{i}", [64, GS], F32) for i in range(2)]
                qbf_sb = [sbt(ps2, f"qbf{i}", [128, GS], BF16) for i in range(2)]
                wqh_sb = [sbt(ps2, f"wqh{i}", [128, E], BF16) for i in range(2)]

                ctld = prog.op("scalar", lambda e: e.dma_start(out=cosT_sb[:], in_=cosT_d),
                               deps=p1f, dma=True, tag="ctld")
                stld = prog.op("scalar", lambda e: e.dma_start(out=sinT_sb[:], in_=sinT_d),
                               deps=[ctld], dma=True, tag="stld")
                rope_c = [ctld, stld]

                sm_t = {0: (PS[0:16, 0:512], BK[5]), 1: (PS[0:16, 512:1024], BK[6]),
                        2: (PQ[0:16, 0:512], BK[4]), 3: (PO[0:16, 0:512], BK[7])}
                kh_ready = vh_ready = None
                prev_chain_last = None
                prev_ln1 = None
                prev_vdc = None
                for gi, outbf in enumerate([khT_bf, vhT_bf]):
                    off = 0 if gi == 0 else 16
                    evs = {}
                    for nm, wT in (("pa", wgT_sb), ("pb", wkvhT_sb)):
                        def smm(e, wT=wT, off=off):
                            for cc in range(4):
                                for ec in range(16):
                                    inst = e.matmul(
                                        sm_t[cc][0],
                                        wT[:, ec * 32 + off: ec * 32 + off + 16],
                                        xnT_sb[:, ec * N + cc * 512: ec * N + cc * 512 + 512],
                                        start=(ec == 0), stop=(ec == 15))
                            return inst
                        extra = [wgtp, kvhtp]
                        if prev_chain_last is not None:
                            extra = extra + [prev_chain_last, prev_ln1, prev_vdc]
                        mm = prog.op("tensor", smm,
                                     deps=rw([], [BK[4], BK[5], BK[6], BK[7]], extra=extra),
                                     tag=f"sm{gi}{nm}")
                        for cc in range(4):
                            sm_t[cc][1].on_write(mm)
                        ev = None
                        for cc in range(4):
                            ev = prog.op("vector",
                                         lambda e, nm=nm, cc=cc: e.tensor_copy(
                                             G[nm][:, cc * 512:(cc + 1) * 512],
                                             sm_t[cc][0]),
                                         deps=([mm] if cc == 0 else []),
                                         tag=f"sme{gi}{nm}{cc}")
                            sm_t[cc][1].on_read(ev)
                        evs[nm] = ev
                    small_evac = {"pa": evs["pa"], "pb": evs["pb"]}
                    gnm, pre = "pa", "pb"
                    prog.op("scalar",
                            lambda e, gnm=gnm: e.activation(G["tmpa"][:], G[gnm][:], AF.Abs),
                            deps=[small_evac[gnm]] +
                                 ([prev_chain_last] if prev_chain_last else []),
                            tag=f"abs{gi}")
                    prog.op("scalar", lambda e: e.activation(G["tmpa"][:], G["tmpa"][:],
                                                             AF.Exp, scale=-1.0), tag=f"ex{gi}")
                    ln1 = prog.op("scalar", lambda e: e.activation(G["tmpa"][:], G["tmpa"][:],
                                                                   AF.Ln, bias=1.0), tag=f"ln{gi}")
                    prog.op("vector",
                            lambda e, gnm=gnm: e.tensor_scalar(
                                G["lgs"][:], G[gnm][:], 0.0, None, OP.min, OP.bypass),
                            deps=[small_evac[gnm]] +
                                 ([prev_chain_last] if prev_chain_last else []),
                            tag=f"mn{gi}")
                    vsub = prog.op("vector", lambda e: e.tensor_tensor(
                        G["lgs"][:], G["lgs"][:], G["tmpa"][:], OP.subtract),
                        deps=[ln1], tag=f"ls{gi}")
                    prog.op("vector", lambda e: e.memset(G["tmpb"][:, 0:1], 0.0), tag=f"z{gi}")
                    prog.op("vector", lambda e: e.tensor_copy(G["tmpb"][:, 1:N],
                                                              G["lgs"][:, 0:N - 1]), tag=f"sh{gi}")
                    fsc = prog.op("vector", lambda e: e.tensor_tensor_scan(
                        G["Fx"][:], G["tmpb"][:], G["tmpb"][:], 0.0, OP.add, OP.bypass),
                        tag=f"fs{gi}")
                    sef = prog.op("scalar", lambda e: e.activation(G["tmpa"][:], G["Fx"][:], AF.Exp),
                                  deps=[fsc, vsub], tag=f"ef{gi}")
                    vdc = prog.op("vector",
                                  lambda e, pre=pre: e.tensor_tensor(
                                      G["tmpb"][:], G[pre][:], G["tmpa"][:], OP.mult),
                                  deps=[sef, small_evac[pre]], tag=f"dc{gi}")
                    prog.op("vector",
                            lambda e: e.tensor_tensor_scan(
                                G["lgs"][:], G["tmpb"][:], G["tmpb"][:], 0.0,
                                OP.add, OP.bypass),
                            tag=f"cs{gi}")
                    if gi == 0:
                        cpy = prog.op("vector",
                                      lambda e, outbf=outbf: e.tensor_scalar(
                                          outbf[:], G["lgs"][:], float(1.0 / np.sqrt(D)),
                                          None, OP.mult, OP.bypass),
                                      tag=f"obf{gi}")
                        kh_ready = cpy
                    else:
                        cpy = prog.op("vector",
                                      lambda e, outbf=outbf: e.tensor_copy(outbf[:], G["lgs"][:]),
                                      tag=f"obf{gi}")
                        vh_ready = cpy
                    prev_chain_last = cpy
                    prev_ln1 = ln1
                    prev_vdc = vdc
                vh_tp = prog.op("sync",
                                lambda e: e.dma_start_transpose(
                                    vhtok_sb[:].rearrange("p (c j) -> p c j", c=NT), vhT_bf[:]),
                                deps=[vh_ready], dma=True, xpose=True, tag="vhtp")
                prog.op("vector",
                        lambda e: e.tensor_copy(vhtok32_sb[:], vhtok_sb[:]),
                        deps=[vh_tp], tag="vhtp32")

                RWQ = [Region() for _ in range(2)]
                RQB = [Region() for _ in range(2)]
                RRT = Region()
                for h in range(H):
                    hb = h % 2
                    wql = prog.op("scalar",
                                  lambda e, h=h, b=hb: e.dma_start(out=wqh_sb[b][:],
                                                                   in_=wqT_s[h]),
                                  deps=rw([], [RWQ[hb]], extra=(p1f if h < 2 else [])),
                                  dma=True, tag=f"wql{h}")
                    commit(wql, [], [RWQ[hb]])
                    for w in range(4):
                        qb = w % 2

                        def qmm(e, b=hb, w=w):
                            for ec in range(16):
                                inst = e.matmul(pslice(2048, 2560),
                                                wqh_sb[b][:, ec * 128:(ec + 1) * 128],
                                                xnT_sb[:, ec * N + w * GS: ec * N + w * GS + GS],
                                                start=(ec == 0), stop=(ec == 15))
                            return inst
                        qp = prog.op("tensor", qmm,
                                     deps=rw([RWQ[hb]], [BK[4]], extra=[wql]),
                                     tag=f"qp{h}_{w}")
                        commit(qp, [RWQ[hb]], [BK[4]])
                        prog.op("vector", lambda e, w=w: e.tensor_tensor(
                            rtmp_sb[0][:, :], PQ[0:64, 0:GS],
                            cosT_sb[:, w * GS:(w + 1) * GS], OP.mult),
                            deps=rw([BK[4], RRT], [], extra=[qp] + rope_c),
                            tag=f"rq{h}_{w}a")
                        prog.op("vector", lambda e, w=w: e.tensor_tensor(
                            rtmp_sb[1][:, :], PQ[64:128, 0:GS],
                            sinT_sb[:, w * GS:(w + 1) * GS], OP.mult), tag=f"rq{h}_{w}b")
                        prog.op("vector", lambda e, b=qb: e.tensor_tensor(
                            qbf_sb[b][0:64, :], rtmp_sb[0][:, :], rtmp_sb[1][:, :],
                            OP.subtract), deps=rw([], [RQB[qb]]), tag=f"rq{h}_{w}c")
                        prog.op("vector", lambda e, w=w: e.tensor_tensor(
                            rtmp_sb[0][:, :], PQ[64:128, 0:GS],
                            cosT_sb[:, w * GS:(w + 1) * GS], OP.mult), tag=f"rq{h}_{w}d")
                        prog.op("vector", lambda e, w=w: e.tensor_tensor(
                            rtmp_sb[1][:, :], PQ[0:64, 0:GS],
                            sinT_sb[:, w * GS:(w + 1) * GS], OP.mult), tag=f"rq{h}_{w}e")
                        rq = prog.op("vector", lambda e, b=qb: e.tensor_tensor(
                            qbf_sb[b][64:128, :], rtmp_sb[0][:, :], rtmp_sb[1][:, :],
                            OP.add), tag=f"rq{h}_{w}f")
                        commit(rq, [BK[4]], [RQB[qb]])
                        RRT.on_write(rq)
                        qs = prog.op("gpsimd",
                                     lambda e, h=h, w=w, b=qb: e.dma_start(
                                         out=qT_s[h][:, w * GS:(w + 1) * GS],
                                         in_=qbf_sb[b][:]),
                                     deps=rw([RQB[qb]], []), dma=True, tag=f"qs{h}_{w}")
                        commit(qs, [RQB[qb]], [])
                        qst[(h, w)] = qs

                pin = prog.fence_all()
                for o in pin:
                    o.needed = True
                emitted = prog.emit_phase(block, 0)

        prep_units = {"w0": [], "w1": [], "wo": []}
        prep_gate = {}
        RP = {k: [Region() for _ in range(2)] for k in ("w0", "w1", "wo")}

        def prep_round(chain, store_eng, bufs, ex, load_src, store_dst, store_src_of,
                       ridx, tag):
            b = ridx % 2

            def rec():
                lw = prog.op("gpsimd", lambda e: e.dma_start(out=bufs[b][:], in_=load_src),
                             deps=rw([], [RP[chain][b]]), dma=True, tag=f"p{tag}l")
                commit(lw, [], [RP[chain][b]])
                sw = prog.op(store_eng, lambda e: e.dma_start(out=store_dst,
                                                              in_=store_src_of(bufs[b])),
                             deps=[lw], dma=True, tag=f"p{tag}s")
                commit(sw, [RP[chain][b]], [])
                prep_gate[(chain, ex, tag)] = sw
                return sw
            return rec

        def build_prep():
            src4 = lambda buf: buf[:].rearrange("p (mb m) -> p mb m", mb=4)
            src8 = lambda buf: buf[:].rearrange("p (dt d) -> p dt d", dt=8)
            for ex in range(NEXP):
                ridx = {"w0": 0, "w1": 0, "wo": 0}
                for mh in range(2):
                    for ec in range(16):
                        for q in range(2):
                            col0 = mh * 2048 + q * 1024
                            src0 = w0_d[ex][ec * 128:(ec + 1) * 128, col0:col0 + 1024]
                            src1 = w1_d[ex][ec * 128:(ec + 1) * 128, col0:col0 + 1024]
                            mb0 = mh * 8 + q * 4
                            dst0 = w0b_s[ex][mb0:mb0 + 4, :, ec * 256:(ec + 1) * 256] \
                                .rearrange("mb p m -> p mb m")
                            dst1 = w1b_s[ex][mb0:mb0 + 4, :, ec * 256:(ec + 1) * 256] \
                                .rearrange("mb p m -> p mb m")
                            tag0 = f"{mh}_{ec}_{q}"
                            prep_units["w0"].append(
                                (6, prep_round("w0", "sync", pw0, ex, src0, dst0,
                                               src4, ridx["w0"], tag0)))
                            ridx["w0"] += 1
                            prep_units["w1"].append(
                                (6, prep_round("w1", "sync", pw1, ex, src1, dst1,
                                               src4, ridx["w1"], tag0)))
                            ridx["w1"] += 1
                for mc in range(32):
                    for dh in range(2):
                        src = wo_d[ex][mc * 128:(mc + 1) * 128, dh * 1024:(dh + 1) * 1024]
                        dst = wob_s[ex][dh * 8:(dh + 1) * 8, :, mc * 128:(mc + 1) * 128] \
                            .rearrange("dt p d -> p dt d")
                        prep_units["wo"].append(
                            (6, prep_round("wo", "sync", pwo, ex, src, dst,
                                           src8, ridx["wo"], f"{mc}_{dh}")))
                        ridx["wo"] += 1
        build_prep()

        def prep_dep(chain, ex, tag):
            dep = prep_gate.get((chain, ex, tag))
            assert dep is not None, ("prep gate not yet recorded", chain, ex, tag)
            return dep

        nhead = {}
        for ch in ("w0", "w1", "wo"):
            n = min(PREP_HEAD, len(prep_units[ch]))
            for _, fn in prep_units[ch][:n]:
                fn()
            nhead[ch] = n

        jn = []
        for e in ENGINES:
            jn.append(prog.op(e, lambda eng: eng.nop(), deps=pin, tag=f"join_{e}"))
        for bk in BK:
            bk.w = None
            bk.readers = []

        with ExitStack() as afs:
            mask4_sb = sbt(afs, "mask4_sb", [128, 2048], BF16)
            maskT4_sb = sbt(afs, "maskT4_sb", [128, 2048], BF16)
            d16_sb = sbt(afs, "d16_sb", [16, 16 * 128], BF16)
            qh_sb = [sbt(afs, f"qh{i}", [128, GS], BF16) for i in range(2)]
            khK_sb = sbt(afs, "khK_sb", [128, N], BF16)
            pt_sb = sbt(afs, "pt_sb", [128, 1024], BF16)
            stmp_sb = sbt(afs, "stmp_sb", [128, 1024], F32)
            trash_sb = sbt(afs, "trash_sb", [128, 1024], BF16)
            lsb_sb = sbt(afs, "lsb_sb", [128, 512], F32)
            o_sb = [sbt(afs, f"o_sb{i}", [128, GS], BF16) for i in range(2)]
            oTh_sb = [sbt(afs, f"oTh{i}", [128, GS], BF16) for i in range(2)]
            woh_sb = [sbt(afs, f"woh{i}", [128, GS], BF16) for i in range(2)]
            ab_sb = [sbt(afs, f"ab{i}", [128, GS], F32) for i in range(2)]
            nmx4_sb = sbt(afs, "nmx4_sb", [128, 4], F32)
            negmx_sb = sbt(afs, "negmx_sb", [128, 4], F32)
            sume4_sb = sbt(afs, "sume4_sb", [128, 4], F32)
            sume_sb = sbt(afs, "sume_sb", [128, 4], F32)
            lsum_sb = sbt(afs, "lsum_sb", [128, 4], F32)
            lrow2_sb = sbt(afs, "lrow2_sb", [1, 512], F32)
            xnh_sb = sbt(afs, "xnh_sb", [128, 16 * HALF], BF16)
            inter_sb = sbt(afs, "inter_sb", [128, 16 * HALF], BF16)
            w0_sb = [sbt(afs, f"w0b{i}", [128, 16 * 256], BF16) for i in range(2)]
            w1_sb = [sbt(afs, f"w1b{i}", [128, 16 * 256], BF16) for i in range(2)]
            wo_sb = [sbt(afs, f"wob{i}", [128, 16 * 128], BF16) for i in range(2)]
            cbc_sb = sbt(afs, "cbc_sb", [128, HALF], BF16)
            silu_sb = sbt(afs, "silu_sb", [128, HALF], F32)
            wev_sb = sbt(afs, "wev_sb", [128, HALF], F32)
            fcol_sb = sbt(afs, "fcol_sb", [128, 8 * 128], F32)
            axf_sb = sbt(afs, "axf_sb", [128, E], F32)
            xf_sb = sbt(afs, "xf_sb", [128, E], F32)

            RQH = [Region() for _ in range(2)]
            RKHK = Region()
            RPT = Region()
            RSTMP = Region()
            RTRASH = Region()
            RLSB = Region()
            RO = [Region() for _ in range(2)]
            ROTH = [Region() for _ in range(2)]
            RWOH = [Region() for _ in range(2)]
            RAB = [Region() for _ in range(2)]
            RST1 = Region()
            RLR = Region()
            RXH = Region()
            RW0 = [Region() for _ in range(2)]
            RW1 = [Region() for _ in range(2)]
            RWO = [Region() for _ in range(2)]
            RCB = Region()
            RSI = Region()
            RIN = [Region() for _ in range(16)]
            RWE = Region()
            RFA = [Region() for _ in range(16)]
            RFC = Region()
            RAXF = Region()
            RXF = Region()

            ostore = {}
            wst_attn = []
            aconst = []

            SA = []

            def au(est, fn):
                SA.append((est * SA_SCALE, fn))

            def a_consts():
                def rec():
                    for dst, src in ((mask4_sb[:], mask4_d), (maskT4_sb[:], maskT4_d),
                                     (d16_sb[:], d16_d)):
                        aconst.append(prog.op(
                            "sync", lambda e, d=dst, s=src: e.dma_start(out=d, in_=s),
                            dma=True, tag="alda"))
                return rec
            au(5, a_consts())

            cmap = [2560, 3072, 2048, 3584]

            def head_units(g, h):
                hb = (g * H + h) % 2
                nch = g + 1
                mend = GS * (g + 1)
                nmt = 4 * (g + 1)

                def u1():
                    qls = prog.op("sync",
                                  lambda e, b=hb: e.dma_start(
                                      out=qh_sb[b][:], in_=qT_s[h][:, g * GS:(g + 1) * GS]),
                                  deps=rw([], [RQH[hb]]), dma=True, tag=f"qhl{g}_{h}")
                    commit(qls, [], [RQH[hb]])
                    vkh = None
                    for c in range(nch):
                        kb = prog.op("tensor",
                                     lambda e, c=c: e.matmul(
                                         pslice(2048, 2560), d16_sb[:, h * 128:(h + 1) * 128],
                                         khT_bf[:, c * 512:(c + 1) * 512],
                                         start=True, stop=True),
                                     deps=rw([], [BK[4]], extra=(aconst if g == 0 else [])),
                                     tag=f"kb{g}_{h}_{c}")
                        commit(kb, [], [BK[4]])
                        vkh = prog.op("vector",
                                      lambda e, c=c: e.tensor_tensor(
                                          khK_sb[:, c * 512:(c + 1) * 512],
                                          kpT_sb[:, c * 512:(c + 1) * 512],
                                          pslice(2048, 2560), OP.mult),
                                      deps=(rw([], [RKHK], extra=[kb]) if c == 0
                                            else [kb]),
                                      tag=f"vkh{g}_{h}_{c}")
                        if c == nch - 1:
                            commit(vkh, [], [RKHK])
                        BK[4].on_read(vkh)

                def u2(t):
                    def smm(e):
                        for c in range(nch):
                            inst = e.matmul(pslice(cmap[c], cmap[c] + 512),
                                            qh_sb[hb][:, t * 128:(t + 1) * 128],
                                            khK_sb[:, c * 512:(c + 1) * 512],
                                            start=True, stop=True)
                        return inst
                    bset = [BK[[5, 6, 4, 7][c]] for c in range(nch)]
                    s1 = prog.op("tensor", smm,
                                 deps=rw([RQH[hb], RKHK], bset), tag=f"s1_{g}_{h}_{t}")
                    commit(s1, [RQH[hb]], [])
                    commit(s1, [RKHK], [])
                    for bk in bset:
                        bk.on_write(s1)
                    vA = prog.op("vector", lambda e: e.tensor_tensor(
                        pslice(cmap[g], cmap[g] + 512), pslice(cmap[g], cmap[g] + 512),
                        mask4_sb[:, t * 512:(t + 1) * 512], OP.add),
                        deps=[s1] + (aconst if g == 0 and h == 0 else []),
                        tag=f"vmA{g}_{h}_{t}")
                    bset[g].on_read(vA)
                    bset[g].on_write(vA)
                    spans = [(2560, 2560 + 512 * min(nch, 2))]
                    if nch >= 3:
                        spans.append((2048, 2560))
                    if nch >= 4:
                        spans.append((3584, 4096))
                    nsp = len(spans)
                    last = None
                    for i, (a, b2) in enumerate(spans):
                        last = prog.op("vector",
                                       lambda e, i=i, a=a, b2=b2: e.reduce_max(
                                           nmx4_sb[:, i:i + 1], pslice(a, b2), axis=AX.X),
                                       deps=rw([], [RST1], extra=[s1, vA]) if i == 0 else [vA],
                                       tag=f"vmc{g}_{h}_{t}_{i}")
                        for bk in banks(a, b2):
                            bk.on_read(last)
                    commit(last, [], [RST1])
                    vm = prog.op("vector", lambda e: e.reduce_max(
                        negmx_sb[:, t:t + 1], nmx4_sb[:, 0:nsp], axis=AX.X,
                        negate=True), tag=f"vm{g}_{h}_{t}")
                    se = None
                    for i, (a, b2) in enumerate(spans):
                        wid = b2 - a
                        vcp = prog.op("vector",
                                      lambda e, a=a, b2=b2, wid=wid: e.tensor_copy(
                                          stmp_sb[:, 0:wid], pslice(a, b2)),
                                      deps=rw([], [RSTMP],
                                              extra=[vm] if i == 0 else [se]),
                                      tag=f"sec{g}_{h}_{t}_{i}")
                        for bk in banks(a, b2):
                            bk.on_read(vcp)
                        RSTMP.on_write(vcp)
                        se = prog.op("scalar",
                                     lambda e, i=i, wid=wid: e.activation(
                                         trash_sb[:, 0:wid], stmp_sb[:, 0:wid],
                                         AF.Exp, bias=negmx_sb[:, t:t + 1],
                                         accum_out=sume4_sb[:, i:i + 1]),
                                     deps=rw([RTRASH], [], extra=[vm, vcp]),
                                     tag=f"se{g}_{h}_{t}_{i}")
                        RSTMP.on_read(se)
                    RTRASH.on_write(se)
                    RST1.on_read(se)
                    vsm = prog.op("vector", lambda e: e.reduce_sum(
                        sume_sb[:, t:t + 1], sume4_sb[:, 0:nsp], axis=AX.X),
                        deps=[se], tag=f"vsm{g}_{h}_{t}")
                    RST1.on_write(vsm)
                    sl = prog.op("scalar", lambda e: e.activation(
                        lsum_sb[:, t:t + 1], sume_sb[:, t:t + 1], AF.Ln),
                        deps=[vsm], tag=f"sl{g}_{h}_{t}")
                    vl = prog.op("vector", lambda e: e.tensor_tensor(
                        lsum_sb[:, t:t + 1], lsum_sb[:, t:t + 1],
                        negmx_sb[:, t:t + 1], OP.subtract), deps=[sl], tag=f"vl{g}_{h}_{t}")
                    RST1.on_write(vl)

                def u3():
                    tts = []
                    for t in range(4):
                        tt = prog.op("tensor",
                                     lambda e, t=t: e.matmul(
                                         PQ[0:1, t * 128:(t + 1) * 128],
                                         lsum_sb[:, t:t + 1], ident_sb[:],
                                         start=True, stop=True, is_transpose=True),
                                     deps=rw([RST1], [BK[4]]) if t == 0 else [],
                                     tag=f"tt{g}_{h}_{t}")
                        BK[4].on_write(tt)
                        RST1.on_read(tt)
                        tts.append(tt)
                    slr = prog.op("vector",
                                  lambda e: e.tensor_copy(lrow2_sb[:], PQ[0:1, 0:512]),
                                  deps=rw([], [RLR], extra=[tts[3]]), tag=f"slr{g}_{h}")
                    BK[4].on_read(slr)
                    commit(slr, [], [RLR])
                    tlb = prog.op("tensor",
                                  lambda e: e.matmul(pslice(3072, 3584), ones1_sb[:],
                                                     lrow2_sb[:], start=True, stop=True),
                                  deps=rw([RLR], [BK[6]], extra=[slr]), tag=f"tlb{g}_{h}")
                    commit(tlb, [RLR], [BK[6]])
                    slb = prog.op("vector",
                                  lambda e: e.tensor_copy(lsb_sb[:], pslice(3072, 3584)),
                                  deps=rw([], [RLSB], extra=[tlb]), tag=f"slb{g}_{h}")
                    BK[6].on_read(slb)
                    commit(slb, [], [RLSB])

                def u4(r):
                    mts = [2 * r, 2 * r + 1]

                    def p2(e):
                        for mt in mts:
                            pb = mt % 2
                            inst = e.matmul(pslice(2560 + 512 * pb, 3072 + 512 * pb),
                                            khK_sb[:, mt * 128:(mt + 1) * 128],
                                            qh_sb[hb][:, 0:GS], start=True, stop=True)
                        return inst
                    p2o = prog.op("tensor", p2,
                                  deps=rw([RQH[hb], RKHK], [BK[5], BK[6]]),
                                  tag=f"p2_{g}_{h}_{r}")
                    commit(p2o, [RQH[hb]], [])
                    commit(p2o, [RKHK], [])
                    BK[5].on_write(p2o)
                    BK[6].on_write(p2o)
                    dv = [p2o]
                    for mt in mts:
                        if mt >= 4 * g:
                            v = mt - 4 * g
                            pb = mt % 2
                            mk = prog.op("vector",
                                         lambda e, v=v, pb=pb: e.tensor_tensor(
                                             pslice(2560 + 512 * pb, 3072 + 512 * pb),
                                             pslice(2560 + 512 * pb, 3072 + 512 * pb),
                                             maskT4_sb[:, v * 512:(v + 1) * 512], OP.add),
                                         deps=[p2o] + (aconst if g == 0 and h == 0 else []),
                                         tag=f"mk{g}_{h}_{mt}")
                            BK[5 + pb].on_read(mk)
                            BK[5 + pb].on_write(mk)
                            dv.append(mk)
                    vs = None
                    for k in range(2):
                        vs = prog.op("vector",
                                     lambda e, k=k: e.tensor_tensor(
                                         stmp_sb[:, k * 512:(k + 1) * 512],
                                         pslice(2560 + k * 512, 3072 + k * 512),
                                         lsb_sb[:], OP.subtract),
                                     deps=(rw([RLSB], [RSTMP], extra=dv) if k == 0 else []),
                                     tag=f"vs{g}_{h}_{r}{k}")
                        BK[5 + k].on_read(vs)
                        RLSB.on_read(vs)
                    RSTMP.on_write(vs)
                    sx = prog.op("scalar",
                                 lambda e: e.activation(pt_sb[:], stmp_sb[:], AF.Exp),
                                 deps=rw([RSTMP], [RPT]), tag=f"sx{g}_{h}_{r}")
                    RSTMP.on_read(sx)
                    RPT.on_write(sx)
                    vg = None
                    for mt in mts:
                        pb = mt % 2
                        vg = prog.op("vector",
                                     lambda e, mt=mt, pb=pb: e.tensor_scalar(
                                         pt_sb[:, pb * 512:(pb + 1) * 512],
                                         pt_sb[:, pb * 512:(pb + 1) * 512],
                                         vhtok32_sb[:, mt * 16 + h: mt * 16 + h + 1],
                                         None, OP.mult, OP.bypass),
                                     deps=[sx], tag=f"vg{g}_{h}_{mt}")
                        RPT.on_write(vg)

                    def pv(e):
                        for mt in mts:
                            pb = mt % 2
                            inst = e.matmul(pslice(3584, 4096),
                                            vp_sb[:, mt * 128:(mt + 1) * 128],
                                            pt_sb[:, pb * 512:(pb + 1) * 512],
                                            start=(mt == 0), stop=(mt == nmt - 1))
                        return inst
                    pvo = prog.op("tensor", pv,
                                  deps=rw([RPT], [BK[7]] if r == 0 else [], extra=[vg]),
                                  tag=f"pv{g}_{h}_{r}")
                    RPT.on_read(pvo)
                    BK[7].on_write(pvo)

                def u5():
                    so = prog.op("vector",
                                 lambda e, b=hb: e.tensor_copy(o_sb[b][:], pslice(3584, 4096)),
                                 deps=rw([BK[7]], [RO[hb]]),
                                 tag=f"so{g}_{h}")
                    BK[7].on_read(so)
                    commit(so, [], [RO[hb]])
                    st = prog.op("gpsimd",
                                 lambda e, b=hb: e.dma_start(
                                     out=oT_s[h][:, g * GS:(g + 1) * GS], in_=o_sb[b][:]),
                                 deps=[so], dma=True, tag=f"ost{g}_{h}")
                    commit(st, [RO[hb]], [])
                    ostore[(g, h)] = st

                au(6, u1)
                for t in range(4):
                    au(5, lambda t=t: u2(t))
                au(5, u3)
                for r in range(2 * (g + 1)):
                    au(6, lambda r=r: u4(r))
                au(4, u5)

            def wout_units(g):
                for eq in range(4):
                    def quad(eq=eq, hq=0):
                        def rec():
                            for h in range(hq * 4, hq * 4 + 4):
                                b = h % 2
                                ol = prog.op("sync",
                                             lambda e, h=h, b=b: e.dma_start(
                                                 out=oTh_sb[b][:],
                                                 in_=oT_s[h][:, g * GS:(g + 1) * GS]),
                                             deps=rw([], [ROTH[b]],
                                                     extra=[ostore[(g, h)]]),
                                             dma=True, tag=f"otl{g}_{eq}_{h}")
                                commit(ol, [], [ROTH[b]])
                                wl = prog.op("sync",
                                             lambda e, h=h, b=b: e.dma_start(
                                                 out=woh_sb[b][:],
                                                 in_=woutT_s[h][:, eq * 512:(eq + 1) * 512]),
                                             deps=rw([], [RWOH[b]]), dma=True,
                                             tag=f"wol{g}_{eq}_{h}")
                                commit(wl, [], [RWOH[b]])

                                def wmm(e, h=h, b=b):
                                    for tt in range(4):
                                        inst = e.matmul(
                                            pslice(cmap[tt], cmap[tt] + 512),
                                            oTh_sb[b][:, tt * 128:(tt + 1) * 128],
                                            woh_sb[b][:], start=(h == 0), stop=(h == H - 1))
                                    return inst
                                bset = [BK[[5, 6, 4, 7][tt]] for tt in range(4)]
                                wm = prog.op("tensor", wmm,
                                             deps=rw([ROTH[b], RWOH[b]],
                                                     bset if h == 0 else [],
                                                     extra=[ol, wl]),
                                             tag=f"wm{g}_{eq}_{h}")
                                commit(wm, [ROTH[b]], [])
                                commit(wm, [RWOH[b]], [])
                                for bk in bset:
                                    bk.on_write(wm)
                        return rec
                    for hq in range(4):
                        au(12, quad(eq, hq))

                    def evac(eq=eq):
                        for tt in range(4):
                            ba = tt % 2
                            bkr = BK[[5, 6, 4, 7][tt]]
                            ev = prog.op("vector",
                                         lambda e, tt=tt, ba=ba: e.tensor_copy(
                                             ab_sb[ba][:], pslice(cmap[tt], cmap[tt] + 512)),
                                         deps=rw([bkr], [RAB[ba]]),
                                         tag=f"wev{g}_{eq}_{tt}")
                            bkr.on_read(ev)
                            commit(ev, [], [RAB[ba]])
                            st = prog.op("gpsimd",
                                         lambda e, tt=tt, ba=ba: e.dma_start(
                                             out=attn_s[g * GS + tt * 128:
                                                        g * GS + (tt + 1) * 128,
                                                        eq * 512:(eq + 1) * 512],
                                             in_=ab_sb[ba][:]),
                                         deps=[ev], dma=True, tag=f"wst{g}_{eq}_{tt}")
                            commit(st, [RAB[ba]], [])
                            wst_attn.append(st)
                    au(6, evac)

            for g in range(NG):
                for h in range(H):
                    head_units(g, h)
                wout_units(g)

            SF = []

            def fu(est, fn):
                SF.append((est, fn))

            ac_ops = []

            def build_f():
                for half in range(2):
                    def xnh_ld(half=half):
                        def rec():
                            def fn(e):
                                return [e.dma_start(
                                    out=xnh_sb[:].rearrange("p (c n) -> p c n", c=16)
                                    [:, q * 8:(q + 1) * 8, :],
                                    in_=xnT_s[q * 8:(q + 1) * 8, :,
                                              half * HALF:(half + 1) * HALF]
                                    .rearrange("c p n -> p c n")) for q in range(2)]
                            xl = prog.op("scalar", fn, deps=rw([], [RXH]), dma=True,
                                         ndma=2, tag=f"xnh{half}")
                            commit(xl, [], [RXH])
                        return rec
                    fu(15, xnh_ld())
                    for ex in range(NEXP):
                        iex = half * NEXP + ex

                        def cb_fn(half=half, ex=ex):
                            def rec():
                                cb = prog.op(
                                    "tensor",
                                    lambda e: [
                                        e.matmul(pslice(0, 512),
                                                 d8_sb[:, ex * 128:(ex + 1) * 128],
                                                 combT_sb[:, half * HALF: half * HALF + 512],
                                                 start=True, stop=True),
                                        e.matmul(pslice(512, 1024),
                                                 d8_sb[:, ex * 128:(ex + 1) * 128],
                                                 combT_sb[:, half * HALF + 512:
                                                           half * HALF + 1024],
                                                 start=True, stop=True)][-1],
                                    deps=rw([], [BK[0], BK[1]]), tag=f"cb{half}_{ex}")
                                commit(cb, [], [BK[0]])
                                BK[1].on_write(cb)
                                for k in range(2):
                                    scb = prog.op(
                                        "vector",
                                        lambda e, k=k: e.tensor_copy(
                                            cbc_sb[:, k * 512:(k + 1) * 512],
                                            pslice(k * 512, (k + 1) * 512)),
                                        deps=rw([], [RCB], extra=[cb]) if k == 0 else [],
                                        tag=f"scb{half}_{ex}{k}")
                                    BK[k].on_read(scb)
                                commit(scb, [], [RCB])
                            return rec
                        fu(3, cb_fn())

                        for sub in range(2):
                            def mk_l01(mb, sub=sub, ex=ex):
                                bwb = mb % 2
                                mbg = sub * 8 + mb

                                def rec():
                                    dep0 = prep_dep("w0", ex, f"{sub}_15_1")
                                    dep1 = prep_dep("w1", ex, f"{sub}_15_1")
                                    l0 = prog.op("scalar",
                                                 lambda e: e.dma_start(
                                                     out=w0_sb[bwb][:], in_=w0b_s[ex][mbg]),
                                                 deps=rw([], [RW0[bwb]], extra=[dep0]),
                                                 dma=True, tag=f"l0_{iex}_{mbg}")
                                    commit(l0, [], [RW0[bwb]])
                                    l1 = prog.op("scalar",
                                                 lambda e: e.dma_start(
                                                     out=w1_sb[bwb][:], in_=w1b_s[ex][mbg]),
                                                 deps=rw([], [RW1[bwb]], extra=[dep1]),
                                                 dma=True, tag=f"l1_{iex}_{mbg}")
                                    commit(l1, [], [RW1[bwb]])
                                return rec

                            def mk_mt(mb, mi, ld_next, sub=sub, ex=ex, half=half):
                                bwb = mb % 2
                                mtl = mb * 2 + mi

                                def rec():
                                    def h0mm(e):
                                        for ec in range(16):
                                            for c2 in range(2):
                                                inst = e.matmul(
                                                    pslice(c2 * 512, (c2 + 1) * 512),
                                                    w0_sb[bwb][:, ec * 256 + mi * 128:
                                                               ec * 256 + mi * 128 + 128],
                                                    xnh_sb[:, ec * HALF + c2 * 512:
                                                           ec * HALF + c2 * 512 + 512],
                                                    start=(ec == 0), stop=(ec == 15))
                                        return inst
                                    hm0 = prog.op("tensor", h0mm,
                                                  deps=rw([RW0[bwb], RXH], [BK[0], BK[1]]),
                                                  tag=f"h0_{iex}_{sub}_{mtl}")
                                    commit(hm0, [RW0[bwb]], [])
                                    commit(hm0, [RXH], [])
                                    BK[0].on_write(hm0)
                                    BK[1].on_write(hm0)

                                    def h1mm(e):
                                        for ec in range(16):
                                            for c2 in range(2):
                                                inst = e.matmul(
                                                    pslice(1024 + c2 * 512,
                                                           1024 + (c2 + 1) * 512),
                                                    w1_sb[bwb][:, ec * 256 + mi * 128:
                                                               ec * 256 + mi * 128 + 128],
                                                    xnh_sb[:, ec * HALF + c2 * 512:
                                                           ec * HALF + c2 * 512 + 512],
                                                    start=(ec == 0), stop=(ec == 15))
                                        return inst
                                    hm1 = prog.op("tensor", h1mm,
                                                  deps=rw([RW1[bwb], RXH], [BK[2], BK[3]]),
                                                  tag=f"h1_{iex}_{sub}_{mtl}")
                                    commit(hm1, [RW1[bwb]], [])
                                    commit(hm1, [RXH], [])
                                    BK[2].on_write(hm1)
                                    BK[3].on_write(hm1)
                                    si = None
                                    for k in range(2):
                                        si = prog.op(
                                            "scalar",
                                            lambda e, k=k: e.activation(
                                                silu_sb[:, k * 512:(k + 1) * 512],
                                                pslice(k * 512, (k + 1) * 512),
                                                AF.Silu),
                                            deps=(rw([], [RSI], extra=[hm0])
                                                  if k == 0 else [si]),
                                            tag=f"si_{iex}_{sub}_{mtl}{k}")
                                        BK[k].on_read(si)
                                    commit(si, [], [RSI])
                                    v1 = None
                                    for k in range(2):
                                        v1 = prog.op(
                                            "vector",
                                            lambda e, k=k: e.tensor_tensor(
                                                silu_sb[:, k * 512:(k + 1) * 512],
                                                silu_sb[:, k * 512:(k + 1) * 512],
                                                pslice(1024 + k * 512, 1024 + (k + 1) * 512),
                                                OP.mult),
                                            deps=(rw([RSI], [], extra=[si, hm1])
                                                  if k == 0 else [v1]),
                                            tag=f"v1_{iex}_{sub}_{mtl}{k}")
                                        BK[2 + k].on_read(v1)
                                    RSI.on_write(v1)
                                    v2 = prog.op("gpsimd",
                                                 lambda e: e.tensor_tensor(
                                                     inter_sb[:, mtl * HALF:(mtl + 1) * HALF],
                                                     silu_sb[:], cbc_sb[:], OP.mult),
                                                 deps=rw([RSI, RCB], [RIN[mtl]],
                                                         extra=[v1]),
                                                 tag=f"v2_{iex}_{sub}_{mtl}")
                                    commit(v2, [RSI], [])
                                    commit(v2, [RCB], [])
                                    commit(v2, [], [RIN[mtl]])
                                    if ld_next is not None:
                                        ld_next()
                                return rec

                            def mk_lwo(dt, sub=sub, ex=ex):
                                bwo = dt % 2

                                def rec():
                                    depo = prep_dep("wo", ex, f"{sub * 16 + 15}_1")
                                    lw = prog.op(
                                        "scalar",
                                        lambda e: e.dma_start(
                                            out=wo_sb[bwo][:],
                                            in_=wob_s[ex][dt][:, sub * 2048:
                                                             (sub + 1) * 2048]),
                                        deps=rw([], [RWO[bwo]], extra=[depo]),
                                        dma=True, tag=f"lwo{iex}_{sub}_{dt}")
                                    commit(lw, [], [RWO[bwo]])
                                return rec

                            def mk_wo(dt, ld_next, sub=sub, ex=ex, half=half):
                                bwo = dt % 2
                                pb = (dt % 2) * 2

                                def rec():
                                    def womm(e):
                                        for mtl in range(16):
                                            for c2 in range(2):
                                                inst = e.matmul(
                                                    pslice(pb * 512 + c2 * 512,
                                                           pb * 512 + (c2 + 1) * 512),
                                                    wo_sb[bwo][:, mtl * 128:(mtl + 1) * 128],
                                                    inter_sb[:, mtl * HALF + c2 * 512:
                                                             mtl * HALF + c2 * 512 + 512],
                                                    start=(mtl == 0), stop=(mtl == 15))
                                        return inst
                                    wdeps = rw([RWO[bwo]] + RIN, [BK[pb], BK[pb + 1]])
                                    wm = prog.op("tensor", womm, deps=wdeps,
                                                 tag=f"wo{iex}_{sub}_{dt}")
                                    commit(wm, [RWO[bwo]], [])
                                    for r in RIN:
                                        r.on_read(wm)
                                    BK[pb].on_write(wm)
                                    BK[pb + 1].on_write(wm)
                                    we = None
                                    for k in range(2):
                                        we = prog.op(
                                            "vector",
                                            lambda e, k=k: e.tensor_copy(
                                                wev_sb[:, k * 512:(k + 1) * 512],
                                                pslice(pb * 512 + k * 512,
                                                       pb * 512 + (k + 1) * 512)),
                                            deps=rw([], [RWE], extra=[wm]) if k == 0 else [],
                                            tag=f"we{iex}_{sub}_{dt}{k}")
                                        BK[pb + k].on_read(we)
                                    commit(we, [], [RWE])
                                    first = (ex == 0 and sub == 0)
                                    ac = prog.op(
                                        "gpsimd",
                                        lambda e: e.dma_start(
                                            out=ffnT_s[dt * 128:(dt + 1) * 128,
                                                       half * HALF:(half + 1) * HALF],
                                            in_=wev_sb[:],
                                            **({} if first else {"accum_op": OP.add})),
                                        deps=rw([], [RFA[dt]], extra=[we]),
                                        dma=True, tag=f"ac{iex}_{sub}_{dt}")
                                    commit(ac, [RWE], [RFA[dt]])
                                    ac_ops.append(ac)
                                    if ld_next is not None:
                                        ld_next()
                                return rec

                            fu(2, mk_l01(0))
                            for mb in range(8):
                                for mi in range(2):
                                    ldn = None
                                    if mi == 0 and mb < 7:
                                        ldn = mk_l01(mb + 1)
                                    elif mi == 0 and mb == 7:
                                        ldn = mk_lwo(0)
                                    fu(14, mk_mt(mb, mi, ldn))
                            for dt in range(16):
                                ldn = mk_lwo(dt + 1) if dt < 15 else None
                                fu(8, mk_wo(dt, ldn))
            build_f()

            weave([SA, SF,
                   [u for u in prep_units["w0"][nhead["w0"]:]],
                   [u for u in prep_units["w1"][nhead["w1"]:]],
                   [u for u in prep_units["wo"][nhead["wo"]:]]])

            y_stores = []
            for half in range(2):
                for t in range(8):
                    row0 = half * HALF + t * 128

                    tfs = []
                    for ph in range(2):
                        def fcol_ld(e, row0=row0, ph=ph):
                            return e.dma_start(
                                out=fcol_sb[:].rearrange("p (c n) -> p c n", c=8),
                                in_=ffnT_s[ph * 1024:(ph + 1) * 1024, row0:row0 + 128]
                                .rearrange("(c p) n -> p c n", p=128))
                        fl = prog.op("gpsimd", fcol_ld,
                                     deps=rw([], [RFC], extra=[r.w for r in RFA]),
                                     dma=True, tag=f"fl{half}_{t}_{ph}")
                        commit(fl, [], [RFC])

                        def ftp(e, ph=ph):
                            for c in range(8):
                                inst = e.matmul(
                                    pslice(ph * 1024 + c * 128, ph * 1024 + (c + 1) * 128),
                                    fcol_sb[:, c * 128:(c + 1) * 128], ident_sb[:],
                                    start=True, stop=True, is_transpose=True)
                            return inst
                        tf = prog.op("tensor", ftp,
                                     deps=rw([RFC], banks(ph * 1024, (ph + 1) * 1024),
                                             extra=[fl]),
                                     tag=f"tf{half}_{t}_{ph}")
                        commit(tf, [RFC], [])
                        for bk in banks(ph * 1024, (ph + 1) * 1024):
                            bk.on_write(tf)
                        tfs.append(tf)
                    al = prog.op("gpsimd",
                                 lambda e, row0=row0: e.dma_start(
                                     out=axf_sb[:], in_=attn_s[row0:row0 + 128, :]),
                                 deps=rw([], [RAXF], extra=wst_attn if t == 0 else []),
                                 dma=True, tag=f"al{half}_{t}")
                    commit(al, [], [RAXF])
                    xlo = prog.op("gpsimd",
                                  lambda e, row0=row0: e.dma_start(
                                      out=xf_sb[:], in_=x_d[row0:row0 + 128, :]),
                                  deps=rw([], [RXF]), dma=True, tag=f"xl{half}_{t}")
                    commit(xlo, [], [RXF])
                    tf = tfs[-1]
                    va = None
                    for q in range(4):
                        va = prog.op("vector",
                                     lambda e, q=q: e.tensor_tensor(
                                         axf_sb[:, q * 512:(q + 1) * 512],
                                         pslice(q * 512, (q + 1) * 512),
                                         axf_sb[:, q * 512:(q + 1) * 512], OP.add),
                                     deps=(rw([RAXF], [], extra=tfs + [al]) if q == 0 else []),
                                     tag=f"va{half}_{t}_{q}")
                        BK[q].on_read(va)
                    commit(va, [RAXF], [RAXF])
                    vb = prog.op("vector",
                                 lambda e: e.tensor_tensor(axf_sb[:], axf_sb[:], xf_sb[:],
                                                           OP.add),
                                 deps=rw([RXF], [], extra=[xlo]), tag=f"vb{half}_{t}")
                    commit(vb, [RXF], [RAXF])
                    ys = prog.op("gpsimd",
                                 lambda e, row0=row0: e.dma_start(
                                     out=y_d[row0:row0 + 128, :], in_=axf_sb[:]),
                                 deps=[vb], dma=True, tag=f"ys{half}_{t}")
                    commit(ys, [RAXF], [])
                    y_stores.append(ys)
            prog.op("scalar", lambda e: e.nop(), deps=[y_stores[-1]], tag="fin")
            prog.emit_phase(block, emitted)

    return nc


def _consts():
    bf = ml_dtypes.bfloat16
    inv = (10000.0 ** (-np.arange(0, D, 2, dtype=np.float64) / D))
    ang = np.arange(N, dtype=np.float64)[:, None] * inv[None, :]
    cos = np.cos(ang).astype(np.float32)
    sin = np.sin(ang).astype(np.float32)
    cos_tok = cos.astype(bf)
    sin_tok = sin.astype(bf)
    cosT = cos.T.copy().astype(bf)
    sinT = sin.T.copy().astype(bf)
    p = np.arange(128)[:, None]
    c = np.arange(512)[None, :]
    mask4 = np.zeros((128, 4, 512), np.float32)
    maskT4 = np.zeros((128, 4, 512), np.float32)
    for v in range(4):
        vis = (c < v * 128) | ((c >= v * 128) & (c < (v + 1) * 128) & ((c - v * 128) <= p))
        mask4[:, v, :] = np.where(vis, 0.0, NEG)
        t_ = c // 128
        jj = c % 128
        visT = (t_ > v) | ((t_ == v) & (jj >= p))
        maskT4[:, v, :] = np.where(visT, 0.0, NEG)
    mask4 = mask4.reshape(128, 2048).astype(bf)
    maskT4 = maskT4.reshape(128, 2048).astype(bf)
    d16 = np.zeros((16, 16 * 128), np.float32)
    for h in range(16):
        d16[h, h * 128:(h + 1) * 128] = 1.0
    d8 = np.zeros((8, 8 * 128), np.float32)
    for ex in range(8):
        d8[ex, ex * 128:(ex + 1) * 128] = 1.0
    return {
        "cos_tok": cos_tok, "sin_tok": sin_tok, "cosT": cosT, "sinT": sinT,
        "mask4": mask4, "maskT4": maskT4, "delta16": d16.astype(bf),
        "delta8": d8.astype(bf),
        "ones1": np.ones((1, 128), np.float32),
        "ident": np.eye(128, dtype=np.float32),
    }


def kernel(L0_attn_x, L0_attn_Wq, L0_attn_Wkv, L0_attn_Wkvh, L0_attn_Wg,
           L0_attn_Wout, L0_ffn_gate_kernel, L0_ffn_w0_kernel, L0_ffn_w1_kernel,
           L0_ffn_wo_kernel):
    x = np.ascontiguousarray(np.asarray(L0_attn_x, dtype=np.float32))
    shared = {
        "Wq": np.ascontiguousarray(np.asarray(L0_attn_Wq, np.float32)),
        "Wkv": np.ascontiguousarray(np.asarray(L0_attn_Wkv, np.float32)),
        "Wkvh": np.ascontiguousarray(np.asarray(L0_attn_Wkvh, np.float32)),
        "Wg": np.ascontiguousarray(np.asarray(L0_attn_Wg, np.float32)),
        "Wout": np.ascontiguousarray(np.asarray(L0_attn_Wout, np.float32)),
        "gate": np.ascontiguousarray(np.asarray(L0_ffn_gate_kernel, np.float32)),
        "w0": np.ascontiguousarray(np.asarray(L0_ffn_w0_kernel, np.float32)),
        "w1": np.ascontiguousarray(np.asarray(L0_ffn_w1_kernel, np.float32)),
        "wo": np.ascontiguousarray(np.asarray(L0_ffn_wo_kernel, np.float32)),
    }
    shared.update(_consts())
    nc = build_program()
    in_maps = [dict(shared, x=np.ascontiguousarray(x[i])) for i in range(8)]
    res = run_bass_kernel_spmd(nc, in_maps, core_ids=list(range(8)))
    return np.stack([np.asarray(r["y"], np.float32) for r in res.results], axis=0)


# version 13 (final)
# speedup vs baseline: 1.0435x; 1.0435x over previous
import numpy as np
import ml_dtypes
import concourse.bass as bass
import concourse.mybir as mybir
from concourse.bass_utils import run_bass_kernel_spmd
from contextlib import ExitStack

F32 = mybir.dt.float32
BF16 = mybir.dt.bfloat16
AF = mybir.ActivationFunctionType
OP = mybir.AluOpType
AX = mybir.AxisListType

N = 2048
E = 2048
H = 16
D = 128
NEXP = 8
MLP = 4096
NT = 16
NG = 4
GS = 512
HALF = 1024
NEG = -1.0e30
ENGINES = ("tensor", "vector", "scalar", "gpsimd", "sync")
PREP_HEAD = 64
SA_SCALE = 2.18
WEAVE_LOG = []


class Op:
    __slots__ = ("engine", "fn", "deps", "dma", "ndma", "lane", "ev", "needed", "tag")

    def __init__(self, engine, fn, deps, dma, ndma, tag):
        self.engine = engine
        self.fn = fn
        self.deps = [d for d in deps if d is not None]
        self.dma = dma
        self.ndma = ndma
        self.lane = None
        self.ev = None
        self.needed = False
        self.tag = tag


class Prog:

    NDMA_SEMS = 16

    def __init__(self, nc, ctx):
        self.nc = nc
        self.ops = []
        self.sems = {}
        self.sem_vals = {}
        self.dma_idx = {}
        for e in ENGINES:
            name = f"s_{e}_c"
            self.sems[name] = ctx.enter_context(nc.semaphore(name))
            self.sem_vals[name] = 0
            self.dma_idx[e] = 0
            for i in range(self.NDMA_SEMS):
                name = f"s_{e}_d{i}"
                self.sems[name] = ctx.enter_context(nc.semaphore(name))
                self.sem_vals[name] = 0
        self.last = {e: None for e in ENGINES}
        self.last_compute = {e: None for e in ENGINES}
        self.last_xpose = None
        self.lane_last = {e: [None] * self.NDMA_SEMS for e in ENGINES}

    def op(self, engine, fn, deps=(), dma=False, ndma=1, xpose=False, tag=""):
        o = Op(engine, fn, list(deps), dma, ndma, tag)
        if xpose and self.last_xpose is not None:
            o.deps.append(self.last_xpose)
        if engine in ("vector", "scalar") and not dma \
                and self.last_compute[engine] is not None:
            o.deps.append(self.last_compute[engine])
        if dma:
            o.lane = self.dma_idx[engine]
            self.dma_idx[engine] = (o.lane + 1) % self.NDMA_SEMS
            self.lane_last[engine][o.lane] = o
        else:
            self.last_compute[engine] = o
        if xpose:
            self.last_xpose = o
        self.ops.append(o)
        self.last[engine] = o
        return o

    def fence(self, engine):
        out = [o for o in self.lane_last[engine] if o is not None]
        if self.last_compute[engine] is not None:
            out.append(self.last_compute[engine])
        if self.last[engine] is not None and self.last[engine] not in out:
            out.append(self.last[engine])
        return out

    def fence_all(self):
        return [o for e in ENGINES for o in self.fence(e)]

    def emit_phase(self, block, start_idx):
        ops = self.ops[start_idx:]
        cur = set(id(o) for o in ops)
        for o in ops:
            for d in o.deps:
                if id(d) not in cur and d.ev is None:
                    raise RuntimeError(
                        f"dep {d.tag} of {o.tag} is from an already-emitted phase "
                        f"but was not pinned")
                d.needed = True
        for o in ops:
            if (o.needed or o.dma) and o.ev is None:
                if o.dma:
                    name = f"s_{o.engine}_d{o.lane}"
                    step = 16 * o.ndma
                else:
                    name = f"s_{o.engine}_c"
                    step = 1
                self.sem_vals[name] += step
                o.ev = (name, self.sem_vals[name])
        per_engine = {e: [] for e in ENGINES}
        for o in ops:
            per_engine[o.engine].append(o)
        for e in ENGINES:
            recs = per_engine[e]
            if not recs:
                continue
            sec = getattr(block, e)

            def section(eng, recs=recs):
                seen = {}
                for o in recs:
                    for d in o.deps:
                        if d.ev is None:
                            raise RuntimeError(f"dep of {o.tag} has no event: {d.tag}")
                        s, v = d.ev
                        if seen.get(s, 0) < v:
                            eng.wait_ge(self.sems[s], v)
                            seen[s] = v
                    if o.dma and o.ev is not None:
                        s, v = o.ev
                        prev = v - 16 * o.ndma
                        if prev > 0 and seen.get(s, 0) < prev:
                            eng.wait_ge(self.sems[s], prev)
                            seen[s] = prev
                    inst = o.fn(eng)
                    if o.ev is not None:
                        insts = inst if isinstance(inst, list) else [inst]
                        assert len(insts) == (o.ndma if o.dma else 1), o.tag
                        for i2 in insts:
                            i2.then_inc(self.sems[o.ev[0]], 16 if o.dma else 1)

            sec(section)
        return len(self.ops)


class Region:

    def __init__(self):
        self.w = None
        self.readers = []

    def wdeps(self):
        return ([self.w] if self.w is not None else []) + self.readers

    def on_write(self, op):
        self.w = op
        self.readers = []

    def rdeps(self):
        return [self.w] if self.w is not None else []

    def on_read(self, op):
        self.readers.append(op)


def rw(regions_read, regions_written, extra=()):
    deps = list(extra)
    for r in regions_read:
        deps += r.rdeps()
    for r in regions_written:
        deps += r.wdeps()
    return deps


def commit(op, regions_read, regions_written):
    for r in regions_read:
        r.on_read(op)
    for r in regions_written:
        r.on_write(op)
    return op


def weave(streams):
    clocks = [0.0] * len(streams)
    idx = [0] * len(streams)
    while True:
        cands = [i for i in range(len(streams)) if idx[i] < len(streams[i])]
        if not cands:
            break
        i = min(cands, key=lambda k: clocks[k])
        est, fn = streams[i][idx[i]]
        idx[i] += 1
        fn()
        clocks[i] += est
        WEAVE_LOG.append((i, est))


def build_program():
    nc = bass.Bass("TRN2", target_bir_lowering=False)

    x_d = nc.dram_tensor("x", [N, E], F32, kind="ExternalInput").ap()
    wq_d = nc.dram_tensor("Wq", [E, E], F32, kind="ExternalInput").ap()
    wkv_d = nc.dram_tensor("Wkv", [256, E], F32, kind="ExternalInput").ap()
    wkvh_d = nc.dram_tensor("Wkvh", [32, E], F32, kind="ExternalInput").ap()
    wg_d = nc.dram_tensor("Wg", [32, E], F32, kind="ExternalInput").ap()
    wout_d = nc.dram_tensor("Wout", [E, E], F32, kind="ExternalInput").ap()
    gate_d = nc.dram_tensor("gate", [E, NEXP], F32, kind="ExternalInput").ap()
    w0_d = nc.dram_tensor("w0", [NEXP, E, MLP], F32, kind="ExternalInput").ap()
    w1_d = nc.dram_tensor("w1", [NEXP, E, MLP], F32, kind="ExternalInput").ap()
    wo_d = nc.dram_tensor("wo", [NEXP, MLP, E], F32, kind="ExternalInput").ap()
    costk_d = nc.dram_tensor("cos_tok", [N, 64], BF16, kind="ExternalInput").ap()
    sintk_d = nc.dram_tensor("sin_tok", [N, 64], BF16, kind="ExternalInput").ap()
    cosT_d = nc.dram_tensor("cosT", [64, N], BF16, kind="ExternalInput").ap()
    sinT_d = nc.dram_tensor("sinT", [64, N], BF16, kind="ExternalInput").ap()
    mask4_d = nc.dram_tensor("mask4", [128, 2048], BF16, kind="ExternalInput").ap()
    maskT4_d = nc.dram_tensor("maskT4", [128, 2048], BF16, kind="ExternalInput").ap()
    d16_d = nc.dram_tensor("delta16", [16, 16 * 128], BF16, kind="ExternalInput").ap()
    d8_d = nc.dram_tensor("delta8", [8, 8 * 128], BF16, kind="ExternalInput").ap()
    ones1_d = nc.dram_tensor("ones1", [1, 128], F32, kind="ExternalInput").ap()
    ident_d = nc.dram_tensor("ident", [128, 128], F32, kind="ExternalInput").ap()
    y_d = nc.dram_tensor("y", [N, E], F32, kind="ExternalOutput").ap()

    xnT_s = nc.dram_tensor("xnT_s", [16, 128, N], BF16, kind="Internal").ap()
    wqT_s = nc.dram_tensor("wqT_s", [16, 128, E], BF16, kind="Internal").ap()
    woutT_s = nc.dram_tensor("woutT_s", [16, 128, E], BF16, kind="Internal").ap()
    w0b_s = nc.dram_tensor("w0b_s", [NEXP, 16, 128, 4096], BF16, kind="Internal").ap()
    w1b_s = nc.dram_tensor("w1b_s", [NEXP, 16, 128, 4096], BF16, kind="Internal").ap()
    wob_s = nc.dram_tensor("wob_s", [NEXP, 16, 128, 4096], BF16, kind="Internal").ap()
    qT_s = nc.dram_tensor("qT_s", [16, 128, N], BF16, kind="Internal").ap()
    oT_s = nc.dram_tensor("oT_s", [16, 128, N], BF16, kind="Internal").ap()
    attn_s = nc.dram_tensor("attn_s", [N, E], F32, kind="Internal").ap()
    ffnT_s = nc.dram_tensor("ffnT_s", [E, N], F32, kind="Internal").ap()

    with ExitStack() as root:
        def sbt(ctx, name, shape, dt):
            return ctx.enter_context(nc.sbuf_tensor(name, shape, dt))

        PSF = [root.enter_context(nc.psum_tensor(f"psf{b}", [128, 512], F32))
               for b in range(4)]
        PQ = root.enter_context(nc.psum_tensor("psq", [128, 512], F32))
        PS = root.enter_context(nc.psum_tensor("pss", [128, 1024], F32))
        PO = root.enter_context(nc.psum_tensor("pso", [128, 512], F32))
        BK = [Region() for _ in range(8)]

        def pslice(c0, c1):
            if c1 <= 2048:
                b = c0 // 512
                assert (c1 - 1) // 512 == b, (c0, c1)
                return PSF[b][:, c0 - 512 * b: c1 - 512 * b]
            if c1 <= 2560:
                assert c0 >= 2048
                return PQ[:, c0 - 2048: c1 - 2048]
            if c1 <= 3584:
                assert c0 >= 2560
                return PS[:, c0 - 2560: c1 - 2560]
            assert c0 >= 3584
            return PO[:, c0 - 3584: c1 - 3584]

        def banks(c0, c1):
            return [BK[b] for b in range(c0 // 512, (c1 + 511) // 512)]

        prog = Prog(nc, root)
        block = root.enter_context(nc.Block())

        d8_sb = sbt(root, "d8_sb", [8, 8 * 128], BF16)
        ones1_sb = sbt(root, "ones1_sb", [1, 128], F32)
        ident_sb = sbt(root, "ident_sb", [128, 128], F32)
        combT_sb = sbt(root, "combT_sb", [8, NT * 128], BF16)
        kpT_sb = sbt(root, "kpT_sb", [128, NT * 128], BF16)
        vp_sb = sbt(root, "vp_sb", [128, NT * 128], BF16)
        vhtok_sb = sbt(root, "vhtok_sb", [128, NT * 16], BF16)
        vhtok32_sb = sbt(root, "vhtok32_sb", [128, NT * 16], F32)
        khT_bf = sbt(root, "khT_bf", [16, N], BF16)
        pw0 = [sbt(root, f"pw0_{i}", [128, 1024], BF16) for i in range(2)]
        pw1 = [sbt(root, f"pw1_{i}", [128, 1024], BF16) for i in range(2)]
        pwo = [sbt(root, f"pwo_{i}", [128, 1024], BF16) for i in range(2)]

        qst = {}
        ct_done = []
        wout_st_by_h = {h: [] for h in range(16)}
        xn_store = None

        with ExitStack() as pscope:
            xnT_sb = sbt(pscope, "xnT_sb", [128, 16 * N], BF16)
            wgT_sb = sbt(pscope, "wgT_sb", [128, 16 * 32], BF16)
            wkvhT_sb = sbt(pscope, "wkvhT_sb", [128, 16 * 32], BF16)
            with ExitStack() as ps1:
                x_sb = [sbt(ps1, f"x_sb{i}", [128, E], F32) for i in range(2)]
                xntk_sb = [sbt(ps1, f"xntk{i}", [128, E], BF16) for i in range(2)]
                wld_sb = [sbt(ps1, f"wld{i}", [128, E], BF16) for i in range(2)]
                wstage_sb = [sbt(ps1, "wstage0", [128, E], BF16)]
                wkvT_sb = sbt(ps1, "wkvT_sb", [128, 16 * 256], BF16)
                gate_sb = sbt(ps1, "gate_sb", [128, 16 * 8], BF16)
                costk_sb = sbt(ps1, "costk_sb", [128, NT * 64], BF16)
                sintk_sb = sbt(ps1, "sintk_sb", [128, NT * 64], BF16)
                sumsq_sb = sbt(ps1, "sumsq_sb", [128, 16], F32)
                mean_sb = sbt(ps1, "mean_sb", [128, 1], F32)
                var_sb = sbt(ps1, "var_sb", [128, 1], F32)
                rstd_sb = sbt(ps1, "rstd_sb", [128, 1], F32)
                sq_sb = sbt(ps1, "sq_sb", [128, E], BF16)
                kprot_sb = [sbt(ps1, f"kprot{i}", [128, 128], BF16) for i in range(2)]
                ktmp = [sbt(ps1, f"ktmp{i}", [128, 64], F32) for i in range(2)]
                eq1_sb = sbt(ps1, "eq1_sb", [128, NT * 8], F32)
                eq2_sb = sbt(ps1, "eq2_sb", [128, NT * 8], F32)
                comb_sb = sbt(ps1, "comb_sb", [128, NT * 8], F32)
                dcol_sb = sbt(ps1, "dcol_sb", [128, 16], F32)
                w1c_sb = sbt(ps1, "w1c_sb", [128, 16], F32)
                w2c_sb = sbt(ps1, "w2c_sb", [128, 16], F32)
                m1_sb = sbt(ps1, "m1_sb", [128, 1], F32)
                m2_sb = sbt(ps1, "m2_sb", [128, 1], F32)

                RX = [Region() for _ in range(2)]
                RXN = [Region() for _ in range(2)]
                RWL = [Region() for _ in range(2)]
                RWS = [Region()]
                RKP = [Region() for _ in range(2)]
                Rstats = Region()

                const_ops = []

                def ld(dst, src):
                    o = prog.op("scalar", lambda e, d=dst, s=src: e.dma_start(out=d, in_=s),
                                dma=True, tag="cld")
                    const_ops.append(o)
                    return o
                ld(d8_sb[:], d8_d)
                ld(ones1_sb[:], ones1_d)
                ld(ident_sb[:], ident_d)
                ld(costk_sb[:].rearrange("p (c j) -> p c j", c=NT),
                   costk_d.rearrange("(c p) j -> p c j", p=128))
                ld(sintk_sb[:].rearrange("p (c j) -> p c j", c=NT),
                   sintk_d.rearrange("(c p) j -> p c j", p=128))

                for c in range(16):
                    lop = prog.op("gpsimd",
                                  lambda e, c=c, b=c % 2: e.dma_start(
                                      out=wld_sb[b][:], in_=wq_d[c * 128:(c + 1) * 128, :]),
                                  deps=rw([], [RWL[c % 2]]), dma=True, tag=f"wqld{c}")
                    commit(lop, [], [RWL[c % 2]])
                    top = prog.op("sync",
                                  lambda e, b=c % 2: e.dma_start_transpose(
                                      wstage_sb[0][:].rearrange("p (k d) -> p k d", k=16),
                                      wld_sb[b][:]),
                                  deps=rw([RWL[c % 2]], [RWS[0]]), dma=True, xpose=True,
                                  tag=f"wqtp{c}")
                    commit(top, [RWL[c % 2]], [RWS[0]])
                    sop = prog.op("scalar",
                                  lambda e, c=c: e.dma_start(
                                      out=wqT_s[c], in_=wstage_sb[0][:]),
                                  deps=rw([RWS[0]], []), dma=True, tag=f"wqst{c}")
                    commit(sop, [RWS[0]], [])
                for c in range(16):
                    lop = prog.op("gpsimd",
                                  lambda e, c=c, b=c % 2: e.dma_start(
                                      out=wld_sb[b][:], in_=wout_d[c * 128:(c + 1) * 128, :]),
                                  deps=rw([], [RWL[c % 2]]), dma=True, tag=f"wold{c}")
                    commit(lop, [], [RWL[c % 2]])
                    top = prog.op("sync",
                                  lambda e, b=c % 2: e.dma_start_transpose(
                                      wstage_sb[0][:].rearrange("p (k d) -> p k d", k=16),
                                      wld_sb[b][:]),
                                  deps=rw([RWL[c % 2]], [RWS[0]]), dma=True, xpose=True,
                                  tag=f"wotp{c}")
                    commit(top, [RWL[c % 2]], [RWS[0]])
                    for h in range(16):
                        sop = prog.op("scalar",
                                      lambda e, c=c, h=h: e.dma_start(
                                          out=woutT_s[h][:, c * 128:(c + 1) * 128],
                                          in_=wstage_sb[0][:, h * 128:(h + 1) * 128]),
                                      deps=rw([RWS[0]], []), dma=True, tag=f"wost{c}_{h}")
                        commit(sop, [RWS[0]], [])
                        wout_st_by_h[h].append(sop)

                kv_ld = []
                for c in range(2):
                    lop = prog.op("gpsimd",
                                  lambda e, c=c, b=c % 2: e.dma_start(
                                      out=wld_sb[b][:], in_=wkv_d[c * 128:(c + 1) * 128, :]),
                                  deps=rw([], [RWL[c % 2]]), dma=True, tag=f"kvld{c}")
                    commit(lop, [], [RWL[c % 2]])
                    top = prog.op("sync",
                                  lambda e, c=c, b=c % 2: e.dma_start_transpose(
                                      wkvT_sb[:].rearrange("p (k n) -> p k n", k=16)
                                      [:, :, c * 128:(c + 1) * 128],
                                      wld_sb[b][:]),
                                  deps=rw([RWL[c % 2]], []), dma=True, xpose=True,
                                  tag=f"kvtp{c}")
                    commit(top, [RWL[c % 2]], [])
                    kv_ld.append(top)
                g32 = prog.op("gpsimd", lambda e: e.dma_start(out=wld_sb[0][0:32, :], in_=wg_d),
                              deps=rw([], [RWL[0]]), dma=True, tag="g32")
                commit(g32, [], [RWL[0]])
                kvh32 = prog.op("gpsimd",
                                lambda e: e.dma_start(out=wld_sb[1][0:32, :], in_=wkvh_d),
                                deps=rw([], [RWL[1]]), dma=True, tag="kvh32")
                commit(kvh32, [], [RWL[1]])
                gtld = prog.op("gpsimd",
                               lambda e: e.dma_start(
                                   out=gate_sb[:].rearrange("p (c k) -> p c k", c=16),
                                   in_=gate_d.rearrange("(c p) k -> p c k", p=128)),
                               dma=True, tag="gtld")
                wgtp = prog.op("sync",
                               lambda e: e.dma_start_transpose(
                                   wgT_sb[:].rearrange("p (c k) -> p c k", c=16),
                                   wld_sb[0][0:32, :]),
                               deps=[g32], dma=True, xpose=True, tag="wgtp")
                commit(wgtp, [RWL[0]], [])
                kvhtp = prog.op("sync",
                                lambda e: e.dma_start_transpose(
                                    wkvhT_sb[:].rearrange("p (c k) -> p c k", c=16),
                                    wld_sb[1][0:32, :]),
                                deps=[kvh32], dma=True, xpose=True, tag="kvhtp")
                commit(kvhtp, [RWL[1]], [])

                xn_tp = []
                for t in range(NT):
                    lop = prog.op("scalar",
                                  lambda e, t=t, b=t % 2: e.dma_start(
                                      out=x_sb[b][:], in_=x_d[t * 128:(t + 1) * 128, :]),
                                  deps=rw([], [RX[t % 2]]), dma=True, tag=f"xld{t}")
                    commit(lop, [], [RX[t % 2]])
                    sqo = prog.op("scalar",
                                  lambda e, t=t, b=t % 2: e.activation(
                                      sq_sb[:], x_sb[b][:], AF.Square,
                                      accum_out=sumsq_sb[:, t:t + 1]),
                                  deps=rw([RX[t % 2]], []), tag=f"sq{t}")
                    commit(sqo, [RX[t % 2]], [])
                    vst = prog.op("vector",
                                  lambda e, b=t % 2: e.reduce_sum(mean_sb[:], x_sb[b][:], axis=AX.X),
                                  deps=rw([RX[t % 2]], [Rstats]), tag=f"vm{t}")
                    commit(vst, [RX[t % 2]], [Rstats])
                    prog.op("vector", lambda e: e.tensor_scalar(
                        mean_sb[:], mean_sb[:], 1.0 / E, None, OP.mult, OP.bypass), tag="vm2")
                    prog.op("vector", lambda e: e.tensor_tensor(
                        var_sb[:], mean_sb[:], mean_sb[:], OP.mult), tag="vm3")
                    prog.op("vector",
                            lambda e, t=t: e.scalar_tensor_tensor(
                                var_sb[:], sumsq_sb[:, t:t + 1], 1.0 / E, var_sb[:],
                                OP.mult, OP.subtract),
                            deps=[sqo], tag=f"vvar{t}")
                    vve = prog.op("vector", lambda e: e.tensor_scalar(
                        var_sb[:], var_sb[:], 1e-5, None, OP.add, OP.bypass), tag="veps")
                    prog.op("scalar", lambda e: e.activation(rstd_sb[:], var_sb[:], AF.Ln),
                            deps=[vve], tag=f"sln{t}")
                    sex = prog.op("scalar", lambda e: e.activation(
                        rstd_sb[:], rstd_sb[:], AF.Exp, scale=-0.5), tag=f"sex{t}")
                    vxn = prog.op("vector",
                                  lambda e, b=t % 2: e.tensor_scalar(
                                      xntk_sb[b][:], x_sb[b][:], mean_sb[:, 0:1],
                                      rstd_sb[:, 0:1], OP.subtract, OP.mult),
                                  deps=rw([RX[t % 2], Rstats], [RXN[t % 2]], extra=[sex]),
                                  tag=f"vxn{t}")
                    commit(vxn, [RX[t % 2], Rstats], [RXN[t % 2]])
                    commit(vxn, [], [RX[t % 2]])
                    top = prog.op("sync",
                                  lambda e, t=t, b=t % 2: e.dma_start_transpose(
                                      xnT_sb[:].rearrange("p (c n) -> p c n", c=16)
                                      [:, :, t * 128:(t + 1) * 128],
                                      xntk_sb[b][:]),
                                  deps=rw([RXN[t % 2]], []), dma=True, xpose=True,
                                  tag=f"xntp{t}")
                    commit(top, [RXN[t % 2]], [])
                    xn_tp.append(top)
                xn_tp_fence = prog.fence("sync")

                def xnT_store(e):
                    return [e.dma_start(
                        out=xnT_s[q * 8:(q + 1) * 8].rearrange("c p n -> p c n"),
                        in_=xnT_sb[:].rearrange("p (c n) -> p c n", c=16)
                        [:, q * 8:(q + 1) * 8, :]) for q in range(2)]
                xn_store = prog.op("scalar", xnT_store, deps=xn_tp_fence, dma=True, ndma=2,
                                   tag="xnst")

                dep_xnt = xn_tp_fence
                for t in range(NT):
                    def kvmm(e, t=t):
                        for ec in range(16):
                            inst = e.matmul(pslice(0, 256),
                                            xnT_sb[:, ec * N + t * 128: ec * N + t * 128 + 128],
                                            wkvT_sb[:, ec * 256:(ec + 1) * 256],
                                            start=(ec == 0), stop=(ec == 15))
                        return inst
                    mm = prog.op("tensor", kvmm,
                                 deps=rw([], [BK[0]], extra=dep_xnt + [kv_ld[1]] + const_ops),
                                 tag=f"kv{t}")
                    commit(mm, [], [BK[0]])
                    b = t % 2
                    prog.op("vector", lambda e, t=t: e.tensor_tensor(
                        ktmp[0][:], pslice(0, 64), costk_sb[:, t * 64:(t + 1) * 64], OP.mult),
                        deps=rw([BK[0]], [RKP[b]], extra=[mm]), tag=f"ropek{t}a")
                    prog.op("vector", lambda e, t=t: e.tensor_tensor(
                        ktmp[1][:], pslice(64, 128), sintk_sb[:, t * 64:(t + 1) * 64], OP.mult),
                        tag=f"ropek{t}b")
                    prog.op("vector", lambda e, b=b: e.tensor_tensor(
                        kprot_sb[b][:, 0:64], ktmp[0][:], ktmp[1][:], OP.subtract),
                        tag=f"ropek{t}c")
                    prog.op("vector", lambda e, t=t: e.tensor_tensor(
                        ktmp[0][:], pslice(64, 128), costk_sb[:, t * 64:(t + 1) * 64], OP.mult),
                        tag=f"ropek{t}d")
                    prog.op("vector", lambda e, t=t: e.tensor_tensor(
                        ktmp[1][:], pslice(0, 64), sintk_sb[:, t * 64:(t + 1) * 64], OP.mult),
                        tag=f"ropek{t}e")
                    rp = prog.op("vector", lambda e, b=b: e.tensor_tensor(
                        kprot_sb[b][:, 64:128], ktmp[0][:], ktmp[1][:], OP.add),
                        tag=f"ropek{t}f")
                    commit(rp, [BK[0]], [RKP[b]])
                    vpo = prog.op("vector",
                                  lambda e, t=t: e.tensor_copy(vp_sb[:, t * 128:(t + 1) * 128],
                                                               pslice(128, 256)),
                                  deps=[mm], tag=f"vp{t}")
                    commit(vpo, [BK[0]], [])
                    tp = prog.op("sync",
                                 lambda e, t=t, b=b: e.dma_start_transpose(
                                     kpT_sb[:, t * 128:(t + 1) * 128], kprot_sb[b][:]),
                                 deps=rw([RKP[b]], []), dma=True, xpose=True, tag=f"kptp{t}")
                    commit(tp, [RKP[b]], [])

                comb_parts = []
                for t in range(NT):
                    def gmm(e, t=t):
                        for ec in range(16):
                            inst = e.matmul(pslice(2560, 2568),
                                            xnT_sb[:, ec * N + t * 128: ec * N + t * 128 + 128],
                                            gate_sb[:, ec * 8:(ec + 1) * 8],
                                            start=(ec == 0), stop=(ec == 15))
                        return inst
                    mm = prog.op("tensor", gmm,
                                 deps=rw([], [BK[5]], extra=dep_xnt + [gtld]), tag=f"g8{t}")
                    commit(mm, [], [BK[5]])
                    prog.op("vector", lambda e: e.reduce_max(
                        m1_sb[:], pslice(2560, 2568), axis=AX.X), deps=[mm], tag=f"gv{t}a")
                    prog.op("vector", lambda e, t=t: e.tensor_scalar(
                        eq1_sb[:, t * 8:(t + 1) * 8], pslice(2560, 2568),
                        m1_sb[:, 0:1], None, OP.is_equal, OP.bypass), tag=f"gv{t}b")
                    prog.op("vector", lambda e, t=t: e.scalar_tensor_tensor(
                        comb_sb[:, t * 8:(t + 1) * 8], eq1_sb[:, t * 8:(t + 1) * 8], NEG,
                        pslice(2560, 2568), OP.mult, OP.add), tag=f"gv{t}c")
                    prog.op("vector", lambda e, t=t: e.reduce_max(
                        m2_sb[:], comb_sb[:, t * 8:(t + 1) * 8], axis=AX.X), tag=f"gv{t}d")
                    prog.op("vector", lambda e, t=t: e.tensor_scalar(
                        eq2_sb[:, t * 8:(t + 1) * 8], pslice(2560, 2568),
                        m2_sb[:, 0:1], None, OP.is_equal, OP.bypass), tag=f"gv{t}e")
                    gv = prog.op("vector", lambda e, t=t: e.tensor_tensor(
                        dcol_sb[:, t:t + 1], m1_sb[:], m2_sb[:], OP.subtract), tag=f"gv{t}f")
                    commit(gv, [BK[5]], [])
                    comb_parts.append(gv)
                sig = prog.op("scalar", lambda e: e.activation(w1c_sb[:], dcol_sb[:], AF.Sigmoid),
                              deps=[comb_parts[-1]], tag="sig")
                vw2 = prog.op("vector", lambda e: e.tensor_scalar(
                    w2c_sb[:], w1c_sb[:], -1.0, 1.0, OP.mult, OP.add), deps=[sig], tag="vw2")
                comb_done = []
                for t in range(NT):
                    prog.op("vector",
                            lambda e, t=t: e.tensor_scalar(
                                eq1_sb[:, t * 8:(t + 1) * 8], eq1_sb[:, t * 8:(t + 1) * 8],
                                w1c_sb[:, t:t + 1], None, OP.mult, OP.bypass),
                            deps=[vw2], tag=f"c1{t}")
                    cop = prog.op("vector",
                                  lambda e, t=t: e.scalar_tensor_tensor(
                                      comb_sb[:, t * 8:(t + 1) * 8], eq2_sb[:, t * 8:(t + 1) * 8],
                                      w2c_sb[:, t:t + 1], eq1_sb[:, t * 8:(t + 1) * 8],
                                      OP.mult, OP.add),
                                  tag=f"c2{t}")
                    comb_done.append(cop)
                for t in range(NT):
                    tp = prog.op("tensor",
                                 lambda e, t=t: e.matmul(PO[0:8, 0:128],
                                                         comb_sb[:, t * 8:(t + 1) * 8],
                                                         ident_sb[:], start=True, stop=True,
                                                         is_transpose=True),
                                 deps=rw([], [BK[7]], extra=[comb_done[t]] + const_ops),
                                 tag=f"ctT{t}")
                    commit(tp, [], [BK[7]])
                    cp = prog.op("vector",
                                 lambda e, t=t: e.tensor_copy(combT_sb[:, t * 128:(t + 1) * 128],
                                                              PO[0:8, 0:128]),
                                 deps=[tp], tag=f"ctC{t}")
                    commit(cp, [BK[7]], [])
                    ct_done.append(cp)

            p1f = prog.fence_all()
            with ExitStack() as ps2:
                G = {}
                for nm in ("pa", "pb", "lgs", "Fx", "tmpa", "tmpb"):
                    G[nm] = sbt(ps2, "G_" + nm, [16, N], F32)
                vhT_bf = sbt(ps2, "vhT_bf", [16, N], BF16)
                cosT_sb = sbt(ps2, "cosT_sb", [64, N], BF16)
                sinT_sb = sbt(ps2, "sinT_sb", [64, N], BF16)
                rtmp_sb = [sbt(ps2, f"rtmp{i}", [64, GS], F32) for i in range(2)]
                qbf_sb = [sbt(ps2, f"qbf{i}", [128, GS], BF16) for i in range(2)]
                wqh_sb = [sbt(ps2, f"wqh{i}", [128, E], BF16) for i in range(2)]

                ctld = prog.op("scalar", lambda e: e.dma_start(out=cosT_sb[:], in_=cosT_d),
                               deps=p1f, dma=True, tag="ctld")
                stld = prog.op("scalar", lambda e: e.dma_start(out=sinT_sb[:], in_=sinT_d),
                               deps=[ctld], dma=True, tag="stld")
                rope_c = [ctld, stld]

                sm_t = {0: (PS[0:16, 0:512], BK[5]), 1: (PS[0:16, 512:1024], BK[6]),
                        2: (PQ[0:16, 0:512], BK[4]), 3: (PO[0:16, 0:512], BK[7])}
                kh_ready = vh_ready = None
                prev_chain_last = None
                prev_ln1 = None
                prev_vdc = None
                for gi, outbf in enumerate([khT_bf, vhT_bf]):
                    off = 0 if gi == 0 else 16
                    evs = {}
                    for nm, wT in (("pa", wgT_sb), ("pb", wkvhT_sb)):
                        def smm(e, wT=wT, off=off):
                            for cc in range(4):
                                for ec in range(16):
                                    inst = e.matmul(
                                        sm_t[cc][0],
                                        wT[:, ec * 32 + off: ec * 32 + off + 16],
                                        xnT_sb[:, ec * N + cc * 512: ec * N + cc * 512 + 512],
                                        start=(ec == 0), stop=(ec == 15))
                            return inst
                        extra = [wgtp, kvhtp]
                        if prev_chain_last is not None:
                            extra = extra + [prev_chain_last, prev_ln1, prev_vdc]
                        mm = prog.op("tensor", smm,
                                     deps=rw([], [BK[4], BK[5], BK[6], BK[7]], extra=extra),
                                     tag=f"sm{gi}{nm}")
                        for cc in range(4):
                            sm_t[cc][1].on_write(mm)
                        ev = None
                        for cc in range(4):
                            ev = prog.op("vector",
                                         lambda e, nm=nm, cc=cc: e.tensor_copy(
                                             G[nm][:, cc * 512:(cc + 1) * 512],
                                             sm_t[cc][0]),
                                         deps=([mm] if cc == 0 else []),
                                         tag=f"sme{gi}{nm}{cc}")
                            sm_t[cc][1].on_read(ev)
                        evs[nm] = ev
                    small_evac = {"pa": evs["pa"], "pb": evs["pb"]}
                    gnm, pre = "pa", "pb"
                    prog.op("scalar",
                            lambda e, gnm=gnm: e.activation(G["tmpa"][:], G[gnm][:], AF.Abs),
                            deps=[small_evac[gnm]] +
                                 ([prev_chain_last] if prev_chain_last else []),
                            tag=f"abs{gi}")
                    prog.op("scalar", lambda e: e.activation(G["tmpa"][:], G["tmpa"][:],
                                                             AF.Exp, scale=-1.0), tag=f"ex{gi}")
                    ln1 = prog.op("scalar", lambda e: e.activation(G["tmpa"][:], G["tmpa"][:],
                                                                   AF.Ln, bias=1.0), tag=f"ln{gi}")
                    prog.op("vector",
                            lambda e, gnm=gnm: e.tensor_scalar(
                                G["lgs"][:], G[gnm][:], 0.0, None, OP.min, OP.bypass),
                            deps=[small_evac[gnm]] +
                                 ([prev_chain_last] if prev_chain_last else []),
                            tag=f"mn{gi}")
                    vsub = prog.op("vector", lambda e: e.tensor_tensor(
                        G["lgs"][:], G["lgs"][:], G["tmpa"][:], OP.subtract),
                        deps=[ln1], tag=f"ls{gi}")
                    prog.op("vector", lambda e: e.memset(G["tmpb"][:, 0:1], 0.0), tag=f"z{gi}")
                    prog.op("vector", lambda e: e.tensor_copy(G["tmpb"][:, 1:N],
                                                              G["lgs"][:, 0:N - 1]), tag=f"sh{gi}")
                    fsc = prog.op("vector", lambda e: e.tensor_tensor_scan(
                        G["Fx"][:], G["tmpb"][:], G["tmpb"][:], 0.0, OP.add, OP.bypass),
                        tag=f"fs{gi}")
                    sef = prog.op("scalar", lambda e: e.activation(G["tmpa"][:], G["Fx"][:], AF.Exp),
                                  deps=[fsc, vsub], tag=f"ef{gi}")
                    vdc = prog.op("vector",
                                  lambda e, pre=pre: e.tensor_tensor(
                                      G["tmpb"][:], G[pre][:], G["tmpa"][:], OP.mult),
                                  deps=[sef, small_evac[pre]], tag=f"dc{gi}")
                    prog.op("vector",
                            lambda e: e.tensor_tensor_scan(
                                G["lgs"][:], G["tmpb"][:], G["tmpb"][:], 0.0,
                                OP.add, OP.bypass),
                            tag=f"cs{gi}")
                    if gi == 0:
                        cpy = prog.op("vector",
                                      lambda e, outbf=outbf: e.tensor_scalar(
                                          outbf[:], G["lgs"][:], float(1.0 / np.sqrt(D)),
                                          None, OP.mult, OP.bypass),
                                      tag=f"obf{gi}")
                        kh_ready = cpy
                    else:
                        cpy = prog.op("vector",
                                      lambda e, outbf=outbf: e.tensor_copy(outbf[:], G["lgs"][:]),
                                      tag=f"obf{gi}")
                        vh_ready = cpy
                    prev_chain_last = cpy
                    prev_ln1 = ln1
                    prev_vdc = vdc
                vh_tp = prog.op("sync",
                                lambda e: e.dma_start_transpose(
                                    vhtok_sb[:].rearrange("p (c j) -> p c j", c=NT), vhT_bf[:]),
                                deps=[vh_ready], dma=True, xpose=True, tag="vhtp")
                prog.op("vector",
                        lambda e: e.tensor_copy(vhtok32_sb[:], vhtok_sb[:]),
                        deps=[vh_tp], tag="vhtp32")

                RWQ = [Region() for _ in range(2)]
                RQB = [Region() for _ in range(2)]
                RRT = Region()
                for h in range(H):
                    hb = h % 2
                    wql = prog.op("scalar",
                                  lambda e, h=h, b=hb: e.dma_start(out=wqh_sb[b][:],
                                                                   in_=wqT_s[h]),
                                  deps=rw([], [RWQ[hb]], extra=(p1f if h < 2 else [])),
                                  dma=True, tag=f"wql{h}")
                    commit(wql, [], [RWQ[hb]])
                    for w in range(4):
                        qb = w % 2

                        def qmm(e, b=hb, w=w):
                            for ec in range(16):
                                inst = e.matmul(pslice(2048, 2560),
                                                wqh_sb[b][:, ec * 128:(ec + 1) * 128],
                                                xnT_sb[:, ec * N + w * GS: ec * N + w * GS + GS],
                                                start=(ec == 0), stop=(ec == 15))
                            return inst
                        qp = prog.op("tensor", qmm,
                                     deps=rw([RWQ[hb]], [BK[4]], extra=[wql]),
                                     tag=f"qp{h}_{w}")
                        commit(qp, [RWQ[hb]], [BK[4]])
                        prog.op("vector", lambda e, w=w: e.tensor_tensor(
                            rtmp_sb[0][:, :], PQ[0:64, 0:GS],
                            cosT_sb[:, w * GS:(w + 1) * GS], OP.mult),
                            deps=rw([BK[4], RRT], [], extra=[qp] + rope_c),
                            tag=f"rq{h}_{w}a")
                        prog.op("vector", lambda e, w=w: e.tensor_tensor(
                            rtmp_sb[1][:, :], PQ[64:128, 0:GS],
                            sinT_sb[:, w * GS:(w + 1) * GS], OP.mult), tag=f"rq{h}_{w}b")
                        prog.op("vector", lambda e, b=qb: e.tensor_tensor(
                            qbf_sb[b][0:64, :], rtmp_sb[0][:, :], rtmp_sb[1][:, :],
                            OP.subtract), deps=rw([], [RQB[qb]]), tag=f"rq{h}_{w}c")
                        prog.op("vector", lambda e, w=w: e.tensor_tensor(
                            rtmp_sb[0][:, :], PQ[64:128, 0:GS],
                            cosT_sb[:, w * GS:(w + 1) * GS], OP.mult), tag=f"rq{h}_{w}d")
                        prog.op("vector", lambda e, w=w: e.tensor_tensor(
                            rtmp_sb[1][:, :], PQ[0:64, 0:GS],
                            sinT_sb[:, w * GS:(w + 1) * GS], OP.mult), tag=f"rq{h}_{w}e")
                        rq = prog.op("vector", lambda e, b=qb: e.tensor_tensor(
                            qbf_sb[b][64:128, :], rtmp_sb[0][:, :], rtmp_sb[1][:, :],
                            OP.add), tag=f"rq{h}_{w}f")
                        commit(rq, [BK[4]], [RQB[qb]])
                        RRT.on_write(rq)
                        qs = prog.op("gpsimd",
                                     lambda e, h=h, w=w, b=qb: e.dma_start(
                                         out=qT_s[h][:, w * GS:(w + 1) * GS],
                                         in_=qbf_sb[b][:]),
                                     deps=rw([RQB[qb]], []), dma=True, tag=f"qs{h}_{w}")
                        commit(qs, [RQB[qb]], [])
                        qst[(h, w)] = qs

                pin = prog.fence_all()
                for o in pin:
                    o.needed = True
                emitted = prog.emit_phase(block, 0)

        prep_units = {"w0": [], "w1": [], "wo": []}
        prep_gate = {}
        RP = {k: [Region() for _ in range(2)] for k in ("w0", "w1", "wo")}

        def prep_round(chain, store_eng, bufs, ex, load_src, store_dst, store_src_of,
                       ridx, tag):
            b = ridx % 2

            def rec():
                lw = prog.op("gpsimd", lambda e: e.dma_start(out=bufs[b][:], in_=load_src),
                             deps=rw([], [RP[chain][b]]), dma=True, tag=f"p{tag}l")
                commit(lw, [], [RP[chain][b]])
                sw = prog.op(store_eng, lambda e: e.dma_start(out=store_dst,
                                                              in_=store_src_of(bufs[b])),
                             deps=[lw], dma=True, tag=f"p{tag}s")
                commit(sw, [RP[chain][b]], [])
                prep_gate[(chain, ex, tag)] = sw
                return sw
            return rec

        def build_prep():
            src4 = lambda buf: buf[:].rearrange("p (mb m) -> p mb m", mb=4)
            src8 = lambda buf: buf[:].rearrange("p (dt d) -> p dt d", dt=8)
            for ex in range(NEXP):
                ridx = {"w0": 0, "w1": 0, "wo": 0}
                for mh in range(2):
                    for ec in range(16):
                        for q in range(2):
                            col0 = mh * 2048 + q * 1024
                            src0 = w0_d[ex][ec * 128:(ec + 1) * 128, col0:col0 + 1024]
                            src1 = w1_d[ex][ec * 128:(ec + 1) * 128, col0:col0 + 1024]
                            mb0 = mh * 8 + q * 4
                            dst0 = w0b_s[ex][mb0:mb0 + 4, :, ec * 256:(ec + 1) * 256] \
                                .rearrange("mb p m -> p mb m")
                            dst1 = w1b_s[ex][mb0:mb0 + 4, :, ec * 256:(ec + 1) * 256] \
                                .rearrange("mb p m -> p mb m")
                            tag0 = f"{mh}_{ec}_{q}"
                            prep_units["w0"].append(
                                (6, prep_round("w0", "sync", pw0, ex, src0, dst0,
                                               src4, ridx["w0"], tag0)))
                            ridx["w0"] += 1
                            prep_units["w1"].append(
                                (6, prep_round("w1", "sync", pw1, ex, src1, dst1,
                                               src4, ridx["w1"], tag0)))
                            ridx["w1"] += 1
                for mc in range(32):
                    for dh in range(2):
                        src = wo_d[ex][mc * 128:(mc + 1) * 128, dh * 1024:(dh + 1) * 1024]
                        dst = wob_s[ex][dh * 8:(dh + 1) * 8, :, mc * 128:(mc + 1) * 128] \
                            .rearrange("dt p d -> p dt d")
                        prep_units["wo"].append(
                            (6, prep_round("wo", "sync", pwo, ex, src, dst,
                                           src8, ridx["wo"], f"{mc}_{dh}")))
                        ridx["wo"] += 1
        build_prep()

        def prep_dep(chain, ex, tag):
            dep = prep_gate.get((chain, ex, tag))
            assert dep is not None, ("prep gate not yet recorded", chain, ex, tag)
            return dep

        nhead = {}
        for ch in ("w0", "w1", "wo"):
            n = min(PREP_HEAD, len(prep_units[ch]))
            for _, fn in prep_units[ch][:n]:
                fn()
            nhead[ch] = n

        jn = []
        for e in ENGINES:
            jn.append(prog.op(e, lambda eng: eng.nop(), deps=pin, tag=f"join_{e}"))
        for bk in BK:
            bk.w = None
            bk.readers = []

        with ExitStack() as afs:
            mask4_sb = sbt(afs, "mask4_sb", [128, 2048], BF16)
            maskT4_sb = sbt(afs, "maskT4_sb", [128, 2048], BF16)
            d16_sb = sbt(afs, "d16_sb", [16, 16 * 128], BF16)
            qh_sb = [sbt(afs, f"qh{i}", [128, GS], BF16) for i in range(2)]
            khK_sb = sbt(afs, "khK_sb", [128, N], BF16)
            pt_sb = sbt(afs, "pt_sb", [128, 1024], BF16)
            stmp_sb = sbt(afs, "stmp_sb", [128, 1024], F32)
            trash_sb = sbt(afs, "trash_sb", [128, 1024], BF16)
            lsb_sb = sbt(afs, "lsb_sb", [128, 512], F32)
            o_sb = [sbt(afs, f"o_sb{i}", [128, GS], BF16) for i in range(2)]
            oTh_sb = [sbt(afs, f"oTh{i}", [128, GS], BF16) for i in range(2)]
            woh_sb = [sbt(afs, f"woh{i}", [128, GS], BF16) for i in range(2)]
            ab_sb = [sbt(afs, f"ab{i}", [128, GS], F32) for i in range(2)]
            nmx4_sb = sbt(afs, "nmx4_sb", [128, 4], F32)
            negmx_sb = sbt(afs, "negmx_sb", [128, 4], F32)
            sume4_sb = sbt(afs, "sume4_sb", [128, 4], F32)
            sume_sb = sbt(afs, "sume_sb", [128, 4], F32)
            lsum_sb = sbt(afs, "lsum_sb", [128, 4], F32)
            lrow2_sb = sbt(afs, "lrow2_sb", [1, 512], F32)
            xnh_sb = sbt(afs, "xnh_sb", [128, 16 * HALF], BF16)
            inter_sb = sbt(afs, "inter_sb", [128, 16 * HALF], BF16)
            w0_sb = [sbt(afs, f"w0b{i}", [128, 16 * 256], BF16) for i in range(2)]
            w1_sb = [sbt(afs, f"w1b{i}", [128, 16 * 256], BF16) for i in range(2)]
            wo_sb = [sbt(afs, f"wob{i}", [128, 16 * 128], BF16) for i in range(2)]
            cbc_sb = sbt(afs, "cbc_sb", [128, HALF], BF16)
            silu_sb = sbt(afs, "silu_sb", [128, HALF], F32)
            wev_sb = sbt(afs, "wev_sb", [128, HALF], F32)
            fcol_sb = sbt(afs, "fcol_sb", [128, 8 * 128], F32)
            axf_sb = sbt(afs, "axf_sb", [128, E], F32)
            xf_sb = sbt(afs, "xf_sb", [128, E], F32)

            RQH = [Region() for _ in range(2)]
            RKHK = Region()
            RPT = Region()
            RSTMP = Region()
            RTRASH = Region()
            RLSB = Region()
            RO = [Region() for _ in range(2)]
            ROTH = [Region() for _ in range(2)]
            RWOH = [Region() for _ in range(2)]
            RAB = [Region() for _ in range(2)]
            RST1 = Region()
            RLR = Region()
            RXH = Region()
            RW0 = [Region() for _ in range(2)]
            RW1 = [Region() for _ in range(2)]
            RWO = [Region() for _ in range(2)]
            RCB = Region()
            RSI = Region()
            RIN = [Region() for _ in range(16)]
            RWE = Region()
            RFA = [Region() for _ in range(16)]
            RFC = Region()
            RAXF = Region()
            RXF = Region()

            ostore = {}
            wst_attn = []
            aconst = []

            SA = []

            def au(est, fn):
                SA.append((est * SA_SCALE, fn))

            def a_consts():
                def rec():
                    for dst, src in ((mask4_sb[:], mask4_d), (maskT4_sb[:], maskT4_d),
                                     (d16_sb[:], d16_d)):
                        aconst.append(prog.op(
                            "sync", lambda e, d=dst, s=src: e.dma_start(out=d, in_=s),
                            dma=True, tag="alda"))
                return rec
            au(5, a_consts())

            cmap = [2560, 3072, 2048, 3584]

            def head_units(g, h):
                hb = (g * H + h) % 2
                nch = g + 1
                mend = GS * (g + 1)
                nmt = 4 * (g + 1)

                def u1():
                    qls = prog.op("sync",
                                  lambda e, b=hb: e.dma_start(
                                      out=qh_sb[b][:], in_=qT_s[h][:, g * GS:(g + 1) * GS]),
                                  deps=rw([], [RQH[hb]]), dma=True, tag=f"qhl{g}_{h}")
                    commit(qls, [], [RQH[hb]])
                    vkh = None
                    for c in range(nch):
                        kb = prog.op("tensor",
                                     lambda e, c=c: e.matmul(
                                         pslice(2048, 2560), d16_sb[:, h * 128:(h + 1) * 128],
                                         khT_bf[:, c * 512:(c + 1) * 512],
                                         start=True, stop=True),
                                     deps=rw([], [BK[4]], extra=(aconst if g == 0 else [])),
                                     tag=f"kb{g}_{h}_{c}")
                        commit(kb, [], [BK[4]])
                        vkh = prog.op("vector",
                                      lambda e, c=c: e.tensor_tensor(
                                          khK_sb[:, c * 512:(c + 1) * 512],
                                          kpT_sb[:, c * 512:(c + 1) * 512],
                                          pslice(2048, 2560), OP.mult),
                                      deps=(rw([], [RKHK], extra=[kb]) if c == 0
                                            else [kb]),
                                      tag=f"vkh{g}_{h}_{c}")
                        if c == nch - 1:
                            commit(vkh, [], [RKHK])
                        BK[4].on_read(vkh)

                def u2(t):
                    def smm(e):
                        for c in range(nch):
                            inst = e.matmul(pslice(cmap[c], cmap[c] + 512),
                                            qh_sb[hb][:, t * 128:(t + 1) * 128],
                                            khK_sb[:, c * 512:(c + 1) * 512],
                                            start=True, stop=True)
                        return inst
                    bset = [BK[[5, 6, 4, 7][c]] for c in range(nch)]
                    s1 = prog.op("tensor", smm,
                                 deps=rw([RQH[hb], RKHK], bset), tag=f"s1_{g}_{h}_{t}")
                    commit(s1, [RQH[hb]], [])
                    commit(s1, [RKHK], [])
                    for bk in bset:
                        bk.on_write(s1)
                    vA = prog.op("vector", lambda e: e.tensor_tensor(
                        pslice(cmap[g], cmap[g] + 512), pslice(cmap[g], cmap[g] + 512),
                        mask4_sb[:, t * 512:(t + 1) * 512], OP.add),
                        deps=[s1] + (aconst if g == 0 and h == 0 else []),
                        tag=f"vmA{g}_{h}_{t}")
                    bset[g].on_read(vA)
                    bset[g].on_write(vA)
                    spans = [(2560, 2560 + 512 * min(nch, 2))]
                    if nch >= 3:
                        spans.append((2048, 2560))
                    if nch >= 4:
                        spans.append((3584, 4096))
                    nsp = len(spans)
                    last = None
                    for i, (a, b2) in enumerate(spans):
                        last = prog.op("vector",
                                       lambda e, i=i, a=a, b2=b2: e.reduce_max(
                                           nmx4_sb[:, i:i + 1], pslice(a, b2), axis=AX.X),
                                       deps=rw([], [RST1], extra=[s1, vA]) if i == 0 else [vA],
                                       tag=f"vmc{g}_{h}_{t}_{i}")
                        for bk in banks(a, b2):
                            bk.on_read(last)
                    commit(last, [], [RST1])
                    vm = prog.op("vector", lambda e: e.reduce_max(
                        negmx_sb[:, t:t + 1], nmx4_sb[:, 0:nsp], axis=AX.X,
                        negate=True), tag=f"vm{g}_{h}_{t}")
                    se = None
                    for i, (a, b2) in enumerate(spans):
                        wid = b2 - a
                        vcp = prog.op("vector",
                                      lambda e, a=a, b2=b2, wid=wid: e.tensor_copy(
                                          stmp_sb[:, 0:wid], pslice(a, b2)),
                                      deps=rw([], [RSTMP],
                                              extra=[vm] if i == 0 else [se]),
                                      tag=f"sec{g}_{h}_{t}_{i}")
                        for bk in banks(a, b2):
                            bk.on_read(vcp)
                        RSTMP.on_write(vcp)
                        se = prog.op("scalar",
                                     lambda e, i=i, wid=wid: e.activation(
                                         trash_sb[:, 0:wid], stmp_sb[:, 0:wid],
                                         AF.Exp, bias=negmx_sb[:, t:t + 1],
                                         accum_out=sume4_sb[:, i:i + 1]),
                                     deps=rw([RTRASH], [], extra=[vm, vcp]),
                                     tag=f"se{g}_{h}_{t}_{i}")
                        RSTMP.on_read(se)
                    RTRASH.on_write(se)
                    RST1.on_read(se)
                    vsm = prog.op("vector", lambda e: e.reduce_sum(
                        sume_sb[:, t:t + 1], sume4_sb[:, 0:nsp], axis=AX.X),
                        deps=[se], tag=f"vsm{g}_{h}_{t}")
                    RST1.on_write(vsm)
                    sl = prog.op("scalar", lambda e: e.activation(
                        lsum_sb[:, t:t + 1], sume_sb[:, t:t + 1], AF.Ln),
                        deps=[vsm], tag=f"sl{g}_{h}_{t}")
                    vl = prog.op("vector", lambda e: e.tensor_tensor(
                        lsum_sb[:, t:t + 1], lsum_sb[:, t:t + 1],
                        negmx_sb[:, t:t + 1], OP.subtract), deps=[sl], tag=f"vl{g}_{h}_{t}")
                    RST1.on_write(vl)

                def u3():
                    tts = []
                    for t in range(4):
                        tt = prog.op("tensor",
                                     lambda e, t=t: e.matmul(
                                         PQ[0:1, t * 128:(t + 1) * 128],
                                         lsum_sb[:, t:t + 1], ident_sb[:],
                                         start=True, stop=True, is_transpose=True),
                                     deps=rw([RST1], [BK[4]]) if t == 0 else [],
                                     tag=f"tt{g}_{h}_{t}")
                        BK[4].on_write(tt)
                        RST1.on_read(tt)
                        tts.append(tt)
                    slr = prog.op("vector",
                                  lambda e: e.tensor_copy(lrow2_sb[:], PQ[0:1, 0:512]),
                                  deps=rw([], [RLR], extra=[tts[3]]), tag=f"slr{g}_{h}")
                    BK[4].on_read(slr)
                    commit(slr, [], [RLR])
                    tlb = prog.op("tensor",
                                  lambda e: e.matmul(pslice(3072, 3584), ones1_sb[:],
                                                     lrow2_sb[:], start=True, stop=True),
                                  deps=rw([RLR], [BK[6]], extra=[slr]), tag=f"tlb{g}_{h}")
                    commit(tlb, [RLR], [BK[6]])
                    slb = prog.op("vector",
                                  lambda e: e.tensor_copy(lsb_sb[:], pslice(3072, 3584)),
                                  deps=rw([], [RLSB], extra=[tlb]), tag=f"slb{g}_{h}")
                    BK[6].on_read(slb)
                    commit(slb, [], [RLSB])

                def u4(r):
                    mts = [2 * r, 2 * r + 1]

                    def p2(e):
                        for mt in mts:
                            pb = mt % 2
                            inst = e.matmul(pslice(2560 + 512 * pb, 3072 + 512 * pb),
                                            khK_sb[:, mt * 128:(mt + 1) * 128],
                                            qh_sb[hb][:, 0:GS], start=True, stop=True)
                        return inst
                    p2o = prog.op("tensor", p2,
                                  deps=rw([RQH[hb], RKHK], [BK[5], BK[6]]),
                                  tag=f"p2_{g}_{h}_{r}")
                    commit(p2o, [RQH[hb]], [])
                    commit(p2o, [RKHK], [])
                    BK[5].on_write(p2o)
                    BK[6].on_write(p2o)
                    dv = [p2o]
                    for mt in mts:
                        if mt >= 4 * g:
                            v = mt - 4 * g
                            pb = mt % 2
                            mk = prog.op("vector",
                                         lambda e, v=v, pb=pb: e.tensor_tensor(
                                             pslice(2560 + 512 * pb, 3072 + 512 * pb),
                                             pslice(2560 + 512 * pb, 3072 + 512 * pb),
                                             maskT4_sb[:, v * 512:(v + 1) * 512], OP.add),
                                         deps=[p2o] + (aconst if g == 0 and h == 0 else []),
                                         tag=f"mk{g}_{h}_{mt}")
                            BK[5 + pb].on_read(mk)
                            BK[5 + pb].on_write(mk)
                            dv.append(mk)
                    vs = None
                    for k in range(2):
                        vs = prog.op("vector",
                                     lambda e, k=k: e.tensor_tensor(
                                         stmp_sb[:, k * 512:(k + 1) * 512],
                                         pslice(2560 + k * 512, 3072 + k * 512),
                                         lsb_sb[:], OP.subtract),
                                     deps=(rw([RLSB], [RSTMP], extra=dv) if k == 0 else []),
                                     tag=f"vs{g}_{h}_{r}{k}")
                        BK[5 + k].on_read(vs)
                        RLSB.on_read(vs)
                    RSTMP.on_write(vs)
                    sx = prog.op("scalar",
                                 lambda e: e.activation(pt_sb[:], stmp_sb[:], AF.Exp),
                                 deps=rw([RSTMP], [RPT]), tag=f"sx{g}_{h}_{r}")
                    RSTMP.on_read(sx)
                    RPT.on_write(sx)
                    vg = None
                    for mt in mts:
                        pb = mt % 2
                        vg = prog.op("vector",
                                     lambda e, mt=mt, pb=pb: e.tensor_scalar(
                                         pt_sb[:, pb * 512:(pb + 1) * 512],
                                         pt_sb[:, pb * 512:(pb + 1) * 512],
                                         vhtok32_sb[:, mt * 16 + h: mt * 16 + h + 1],
                                         None, OP.mult, OP.bypass),
                                     deps=[sx], tag=f"vg{g}_{h}_{mt}")
                        RPT.on_write(vg)

                    def pv(e):
                        for mt in mts:
                            pb = mt % 2
                            inst = e.matmul(pslice(3584, 4096),
                                            vp_sb[:, mt * 128:(mt + 1) * 128],
                                            pt_sb[:, pb * 512:(pb + 1) * 512],
                                            start=(mt == 0), stop=(mt == nmt - 1))
                        return inst
                    pvo = prog.op("tensor", pv,
                                  deps=rw([RPT], [BK[7]] if r == 0 else [], extra=[vg]),
                                  tag=f"pv{g}_{h}_{r}")
                    RPT.on_read(pvo)
                    BK[7].on_write(pvo)

                def u5():
                    so = prog.op("vector",
                                 lambda e, b=hb: e.tensor_copy(o_sb[b][:], pslice(3584, 4096)),
                                 deps=rw([BK[7]], [RO[hb]]),
                                 tag=f"so{g}_{h}")
                    BK[7].on_read(so)
                    commit(so, [], [RO[hb]])
                    st = prog.op("gpsimd",
                                 lambda e, b=hb: e.dma_start(
                                     out=oT_s[h][:, g * GS:(g + 1) * GS], in_=o_sb[b][:]),
                                 deps=[so], dma=True, tag=f"ost{g}_{h}")
                    commit(st, [RO[hb]], [])
                    ostore[(g, h)] = st

                au(6, u1)
                for t in range(4):
                    au(5, lambda t=t: u2(t))
                au(5, u3)
                for r in range(2 * (g + 1)):
                    au(6, lambda r=r: u4(r))
                au(4, u5)

            def wout_units(g):
                for eq in range(4):
                    def quad(eq=eq, hq=0):
                        def rec():
                            for h in range(hq * 4, hq * 4 + 4):
                                b = h % 2
                                ol = prog.op("sync",
                                             lambda e, h=h, b=b: e.dma_start(
                                                 out=oTh_sb[b][:],
                                                 in_=oT_s[h][:, g * GS:(g + 1) * GS]),
                                             deps=rw([], [ROTH[b]],
                                                     extra=[ostore[(g, h)]]),
                                             dma=True, tag=f"otl{g}_{eq}_{h}")
                                commit(ol, [], [ROTH[b]])
                                wl = prog.op("sync",
                                             lambda e, h=h, b=b: e.dma_start(
                                                 out=woh_sb[b][:],
                                                 in_=woutT_s[h][:, eq * 512:(eq + 1) * 512]),
                                             deps=rw([], [RWOH[b]]), dma=True,
                                             tag=f"wol{g}_{eq}_{h}")
                                commit(wl, [], [RWOH[b]])

                                def wmm(e, h=h, b=b):
                                    for tt in range(4):
                                        inst = e.matmul(
                                            pslice(cmap[tt], cmap[tt] + 512),
                                            oTh_sb[b][:, tt * 128:(tt + 1) * 128],
                                            woh_sb[b][:], start=(h == 0), stop=(h == H - 1))
                                    return inst
                                bset = [BK[[5, 6, 4, 7][tt]] for tt in range(4)]
                                wm = prog.op("tensor", wmm,
                                             deps=rw([ROTH[b], RWOH[b]],
                                                     bset if h == 0 else [],
                                                     extra=[ol, wl]),
                                             tag=f"wm{g}_{eq}_{h}")
                                commit(wm, [ROTH[b]], [])
                                commit(wm, [RWOH[b]], [])
                                for bk in bset:
                                    bk.on_write(wm)
                        return rec
                    for hq in range(4):
                        au(12, quad(eq, hq))

                    def evac(eq=eq):
                        for tt in range(4):
                            ba = tt % 2
                            bkr = BK[[5, 6, 4, 7][tt]]
                            ev = prog.op("vector",
                                         lambda e, tt=tt, ba=ba: e.tensor_copy(
                                             ab_sb[ba][:], pslice(cmap[tt], cmap[tt] + 512)),
                                         deps=rw([bkr], [RAB[ba]]),
                                         tag=f"wev{g}_{eq}_{tt}")
                            bkr.on_read(ev)
                            commit(ev, [], [RAB[ba]])
                            st = prog.op("gpsimd",
                                         lambda e, tt=tt, ba=ba: e.dma_start(
                                             out=attn_s[g * GS + tt * 128:
                                                        g * GS + (tt + 1) * 128,
                                                        eq * 512:(eq + 1) * 512],
                                             in_=ab_sb[ba][:]),
                                         deps=[ev], dma=True, tag=f"wst{g}_{eq}_{tt}")
                            commit(st, [RAB[ba]], [])
                            wst_attn.append(st)
                    au(6, evac)

            for g in range(NG):
                for h in range(H):
                    head_units(g, h)
                wout_units(g)

            SF = []

            def fu(est, fn):
                SF.append((est, fn))

            ac_ops = []

            def build_f():
                for half in range(2):
                    def xnh_ld(half=half):
                        def rec():
                            def fn(e):
                                return [e.dma_start(
                                    out=xnh_sb[:].rearrange("p (c n) -> p c n", c=16)
                                    [:, q * 8:(q + 1) * 8, :],
                                    in_=xnT_s[q * 8:(q + 1) * 8, :,
                                              half * HALF:(half + 1) * HALF]
                                    .rearrange("c p n -> p c n")) for q in range(2)]
                            xl = prog.op("scalar", fn, deps=rw([], [RXH]), dma=True,
                                         ndma=2, tag=f"xnh{half}")
                            commit(xl, [], [RXH])
                        return rec
                    fu(15, xnh_ld())
                    for ex in range(NEXP):
                        iex = half * NEXP + ex

                        def cb_fn(half=half, ex=ex):
                            def rec():
                                cb = prog.op(
                                    "tensor",
                                    lambda e: [
                                        e.matmul(pslice(0, 512),
                                                 d8_sb[:, ex * 128:(ex + 1) * 128],
                                                 combT_sb[:, half * HALF: half * HALF + 512],
                                                 start=True, stop=True),
                                        e.matmul(pslice(512, 1024),
                                                 d8_sb[:, ex * 128:(ex + 1) * 128],
                                                 combT_sb[:, half * HALF + 512:
                                                           half * HALF + 1024],
                                                 start=True, stop=True)][-1],
                                    deps=rw([], [BK[0], BK[1]]), tag=f"cb{half}_{ex}")
                                commit(cb, [], [BK[0]])
                                BK[1].on_write(cb)
                                for k in range(2):
                                    scb = prog.op(
                                        "vector",
                                        lambda e, k=k: e.tensor_copy(
                                            cbc_sb[:, k * 512:(k + 1) * 512],
                                            pslice(k * 512, (k + 1) * 512)),
                                        deps=rw([], [RCB], extra=[cb]) if k == 0 else [],
                                        tag=f"scb{half}_{ex}{k}")
                                    BK[k].on_read(scb)
                                commit(scb, [], [RCB])
                            return rec
                        fu(3, cb_fn())

                        for sub in range(2):
                            def mk_l01(mb, sub=sub, ex=ex):
                                bwb = mb % 2
                                mbg = sub * 8 + mb

                                def rec():
                                    dep0 = prep_dep("w0", ex, f"{sub}_15_1")
                                    dep1 = prep_dep("w1", ex, f"{sub}_15_1")
                                    l0 = prog.op("scalar",
                                                 lambda e: e.dma_start(
                                                     out=w0_sb[bwb][:], in_=w0b_s[ex][mbg]),
                                                 deps=rw([], [RW0[bwb]], extra=[dep0]),
                                                 dma=True, tag=f"l0_{iex}_{mbg}")
                                    commit(l0, [], [RW0[bwb]])
                                    l1 = prog.op("scalar",
                                                 lambda e: e.dma_start(
                                                     out=w1_sb[bwb][:], in_=w1b_s[ex][mbg]),
                                                 deps=rw([], [RW1[bwb]], extra=[dep1]),
                                                 dma=True, tag=f"l1_{iex}_{mbg}")
                                    commit(l1, [], [RW1[bwb]])
                                return rec

                            def mk_mt(mb, mi, ld_next, sub=sub, ex=ex, half=half):
                                bwb = mb % 2
                                mtl = mb * 2 + mi

                                def rec():
                                    def h0mm(e):
                                        for ec in range(16):
                                            for c2 in range(2):
                                                inst = e.matmul(
                                                    pslice(c2 * 512, (c2 + 1) * 512),
                                                    w0_sb[bwb][:, ec * 256 + mi * 128:
                                                               ec * 256 + mi * 128 + 128],
                                                    xnh_sb[:, ec * HALF + c2 * 512:
                                                           ec * HALF + c2 * 512 + 512],
                                                    start=(ec == 0), stop=(ec == 15))
                                        return inst
                                    hm0 = prog.op("tensor", h0mm,
                                                  deps=rw([RW0[bwb], RXH], [BK[0], BK[1]]),
                                                  tag=f"h0_{iex}_{sub}_{mtl}")
                                    commit(hm0, [RW0[bwb]], [])
                                    commit(hm0, [RXH], [])
                                    BK[0].on_write(hm0)
                                    BK[1].on_write(hm0)

                                    def h1mm(e):
                                        for ec in range(16):
                                            for c2 in range(2):
                                                inst = e.matmul(
                                                    pslice(1024 + c2 * 512,
                                                           1024 + (c2 + 1) * 512),
                                                    w1_sb[bwb][:, ec * 256 + mi * 128:
                                                               ec * 256 + mi * 128 + 128],
                                                    xnh_sb[:, ec * HALF + c2 * 512:
                                                           ec * HALF + c2 * 512 + 512],
                                                    start=(ec == 0), stop=(ec == 15))
                                        return inst
                                    hm1 = prog.op("tensor", h1mm,
                                                  deps=rw([RW1[bwb], RXH], [BK[2], BK[3]]),
                                                  tag=f"h1_{iex}_{sub}_{mtl}")
                                    commit(hm1, [RW1[bwb]], [])
                                    commit(hm1, [RXH], [])
                                    BK[2].on_write(hm1)
                                    BK[3].on_write(hm1)
                                    si = None
                                    for k in range(2):
                                        si = prog.op(
                                            "scalar",
                                            lambda e, k=k: e.activation(
                                                silu_sb[:, k * 512:(k + 1) * 512],
                                                pslice(k * 512, (k + 1) * 512),
                                                AF.Silu),
                                            deps=(rw([], [RSI], extra=[hm0])
                                                  if k == 0 else [si]),
                                            tag=f"si_{iex}_{sub}_{mtl}{k}")
                                        BK[k].on_read(si)
                                    commit(si, [], [RSI])
                                    v1 = None
                                    for k in range(2):
                                        v1 = prog.op(
                                            "vector",
                                            lambda e, k=k: e.tensor_tensor(
                                                silu_sb[:, k * 512:(k + 1) * 512],
                                                silu_sb[:, k * 512:(k + 1) * 512],
                                                pslice(1024 + k * 512, 1024 + (k + 1) * 512),
                                                OP.mult),
                                            deps=rw([RSI], [], extra=[si, hm1])
                                            if k == 0 else [],
                                            tag=f"v1_{iex}_{sub}_{mtl}{k}")
                                        BK[2 + k].on_read(v1)
                                    RSI.on_write(v1)
                                    v2 = prog.op("vector",
                                                 lambda e: e.tensor_tensor(
                                                     inter_sb[:, mtl * HALF:(mtl + 1) * HALF],
                                                     silu_sb[:], cbc_sb[:], OP.mult),
                                                 deps=rw([RSI, RCB], [RIN[mtl]]),
                                                 tag=f"v2_{iex}_{sub}_{mtl}")
                                    commit(v2, [RSI], [])
                                    commit(v2, [RCB], [])
                                    commit(v2, [], [RIN[mtl]])
                                    if ld_next is not None:
                                        ld_next()
                                return rec

                            def mk_lwo(dt, sub=sub, ex=ex):
                                bwo = dt % 2

                                def rec():
                                    depo = prep_dep("wo", ex, f"{sub * 16 + 15}_1")
                                    lw = prog.op(
                                        "scalar",
                                        lambda e: e.dma_start(
                                            out=wo_sb[bwo][:],
                                            in_=wob_s[ex][dt][:, sub * 2048:
                                                             (sub + 1) * 2048]),
                                        deps=rw([], [RWO[bwo]], extra=[depo]),
                                        dma=True, tag=f"lwo{iex}_{sub}_{dt}")
                                    commit(lw, [], [RWO[bwo]])
                                return rec

                            def mk_wo(dt, ld_next, sub=sub, ex=ex, half=half):
                                bwo = dt % 2
                                pb = (dt % 2) * 2

                                def rec():
                                    def womm(e):
                                        for mtl in range(16):
                                            for c2 in range(2):
                                                inst = e.matmul(
                                                    pslice(pb * 512 + c2 * 512,
                                                           pb * 512 + (c2 + 1) * 512),
                                                    wo_sb[bwo][:, mtl * 128:(mtl + 1) * 128],
                                                    inter_sb[:, mtl * HALF + c2 * 512:
                                                             mtl * HALF + c2 * 512 + 512],
                                                    start=(mtl == 0), stop=(mtl == 15))
                                        return inst
                                    wdeps = rw([RWO[bwo]] + RIN, [BK[pb], BK[pb + 1]])
                                    wm = prog.op("tensor", womm, deps=wdeps,
                                                 tag=f"wo{iex}_{sub}_{dt}")
                                    commit(wm, [RWO[bwo]], [])
                                    for r in RIN:
                                        r.on_read(wm)
                                    BK[pb].on_write(wm)
                                    BK[pb + 1].on_write(wm)
                                    we = None
                                    for k in range(2):
                                        we = prog.op(
                                            "vector",
                                            lambda e, k=k: e.tensor_copy(
                                                wev_sb[:, k * 512:(k + 1) * 512],
                                                pslice(pb * 512 + k * 512,
                                                       pb * 512 + (k + 1) * 512)),
                                            deps=rw([], [RWE], extra=[wm]) if k == 0 else [],
                                            tag=f"we{iex}_{sub}_{dt}{k}")
                                        BK[pb + k].on_read(we)
                                    commit(we, [], [RWE])
                                    first = (ex == 0 and sub == 0)
                                    ac = prog.op(
                                        "gpsimd",
                                        lambda e: e.dma_start(
                                            out=ffnT_s[dt * 128:(dt + 1) * 128,
                                                       half * HALF:(half + 1) * HALF],
                                            in_=wev_sb[:],
                                            **({} if first else {"accum_op": OP.add})),
                                        deps=rw([], [RFA[dt]], extra=[we]),
                                        dma=True, tag=f"ac{iex}_{sub}_{dt}")
                                    commit(ac, [RWE], [RFA[dt]])
                                    ac_ops.append(ac)
                                    if ld_next is not None:
                                        ld_next()
                                return rec

                            fu(2, mk_l01(0))
                            for mb in range(8):
                                for mi in range(2):
                                    ldn = None
                                    if mi == 0 and mb < 7:
                                        ldn = mk_l01(mb + 1)
                                    elif mi == 0 and mb == 7:
                                        ldn = mk_lwo(0)
                                    fu(14, mk_mt(mb, mi, ldn))
                            for dt in range(16):
                                ldn = mk_lwo(dt + 1) if dt < 15 else None
                                fu(8, mk_wo(dt, ldn))
            build_f()

            weave([SA, SF,
                   [u for u in prep_units["w0"][nhead["w0"]:]],
                   [u for u in prep_units["w1"][nhead["w1"]:]],
                   [u for u in prep_units["wo"][nhead["wo"]:]]])

            y_stores = []
            for half in range(2):
                for t in range(8):
                    row0 = half * HALF + t * 128

                    tfs = []
                    for ph in range(2):
                        def fcol_ld(e, row0=row0, ph=ph):
                            return e.dma_start(
                                out=fcol_sb[:].rearrange("p (c n) -> p c n", c=8),
                                in_=ffnT_s[ph * 1024:(ph + 1) * 1024, row0:row0 + 128]
                                .rearrange("(c p) n -> p c n", p=128))
                        fl = prog.op("gpsimd", fcol_ld,
                                     deps=rw([], [RFC], extra=[r.w for r in RFA]),
                                     dma=True, tag=f"fl{half}_{t}_{ph}")
                        commit(fl, [], [RFC])

                        def ftp(e, ph=ph):
                            for c in range(8):
                                inst = e.matmul(
                                    pslice(ph * 1024 + c * 128, ph * 1024 + (c + 1) * 128),
                                    fcol_sb[:, c * 128:(c + 1) * 128], ident_sb[:],
                                    start=True, stop=True, is_transpose=True)
                            return inst
                        tf = prog.op("tensor", ftp,
                                     deps=rw([RFC], banks(ph * 1024, (ph + 1) * 1024),
                                             extra=[fl]),
                                     tag=f"tf{half}_{t}_{ph}")
                        commit(tf, [RFC], [])
                        for bk in banks(ph * 1024, (ph + 1) * 1024):
                            bk.on_write(tf)
                        tfs.append(tf)
                    al = prog.op("gpsimd",
                                 lambda e, row0=row0: e.dma_start(
                                     out=axf_sb[:], in_=attn_s[row0:row0 + 128, :]),
                                 deps=rw([], [RAXF], extra=wst_attn if t == 0 else []),
                                 dma=True, tag=f"al{half}_{t}")
                    commit(al, [], [RAXF])
                    xlo = prog.op("gpsimd",
                                  lambda e, row0=row0: e.dma_start(
                                      out=xf_sb[:], in_=x_d[row0:row0 + 128, :]),
                                  deps=rw([], [RXF]), dma=True, tag=f"xl{half}_{t}")
                    commit(xlo, [], [RXF])
                    tf = tfs[-1]
                    va = None
                    for q in range(4):
                        va = prog.op("vector",
                                     lambda e, q=q: e.tensor_tensor(
                                         axf_sb[:, q * 512:(q + 1) * 512],
                                         pslice(q * 512, (q + 1) * 512),
                                         axf_sb[:, q * 512:(q + 1) * 512], OP.add),
                                     deps=(rw([RAXF], [], extra=tfs + [al]) if q == 0 else []),
                                     tag=f"va{half}_{t}_{q}")
                        BK[q].on_read(va)
                    commit(va, [RAXF], [RAXF])
                    vb = prog.op("vector",
                                 lambda e: e.tensor_tensor(axf_sb[:], axf_sb[:], xf_sb[:],
                                                           OP.add),
                                 deps=rw([RXF], [], extra=[xlo]), tag=f"vb{half}_{t}")
                    commit(vb, [RXF], [RAXF])
                    ys = prog.op("gpsimd",
                                 lambda e, row0=row0: e.dma_start(
                                     out=y_d[row0:row0 + 128, :], in_=axf_sb[:]),
                                 deps=[vb], dma=True, tag=f"ys{half}_{t}")
                    commit(ys, [RAXF], [])
                    y_stores.append(ys)
            prog.op("scalar", lambda e: e.nop(), deps=[y_stores[-1]], tag="fin")
            prog.emit_phase(block, emitted)

    return nc


def _consts():
    bf = ml_dtypes.bfloat16
    inv = (10000.0 ** (-np.arange(0, D, 2, dtype=np.float64) / D))
    ang = np.arange(N, dtype=np.float64)[:, None] * inv[None, :]
    cos = np.cos(ang).astype(np.float32)
    sin = np.sin(ang).astype(np.float32)
    cos_tok = cos.astype(bf)
    sin_tok = sin.astype(bf)
    cosT = cos.T.copy().astype(bf)
    sinT = sin.T.copy().astype(bf)
    p = np.arange(128)[:, None]
    c = np.arange(512)[None, :]
    mask4 = np.zeros((128, 4, 512), np.float32)
    maskT4 = np.zeros((128, 4, 512), np.float32)
    for v in range(4):
        vis = (c < v * 128) | ((c >= v * 128) & (c < (v + 1) * 128) & ((c - v * 128) <= p))
        mask4[:, v, :] = np.where(vis, 0.0, NEG)
        t_ = c // 128
        jj = c % 128
        visT = (t_ > v) | ((t_ == v) & (jj >= p))
        maskT4[:, v, :] = np.where(visT, 0.0, NEG)
    mask4 = mask4.reshape(128, 2048).astype(bf)
    maskT4 = maskT4.reshape(128, 2048).astype(bf)
    d16 = np.zeros((16, 16 * 128), np.float32)
    for h in range(16):
        d16[h, h * 128:(h + 1) * 128] = 1.0
    d8 = np.zeros((8, 8 * 128), np.float32)
    for ex in range(8):
        d8[ex, ex * 128:(ex + 1) * 128] = 1.0
    return {
        "cos_tok": cos_tok, "sin_tok": sin_tok, "cosT": cosT, "sinT": sinT,
        "mask4": mask4, "maskT4": maskT4, "delta16": d16.astype(bf),
        "delta8": d8.astype(bf),
        "ones1": np.ones((1, 128), np.float32),
        "ident": np.eye(128, dtype=np.float32),
    }


def kernel(L0_attn_x, L0_attn_Wq, L0_attn_Wkv, L0_attn_Wkvh, L0_attn_Wg,
           L0_attn_Wout, L0_ffn_gate_kernel, L0_ffn_w0_kernel, L0_ffn_w1_kernel,
           L0_ffn_wo_kernel):
    x = np.ascontiguousarray(np.asarray(L0_attn_x, dtype=np.float32))
    shared = {
        "Wq": np.ascontiguousarray(np.asarray(L0_attn_Wq, np.float32)),
        "Wkv": np.ascontiguousarray(np.asarray(L0_attn_Wkv, np.float32)),
        "Wkvh": np.ascontiguousarray(np.asarray(L0_attn_Wkvh, np.float32)),
        "Wg": np.ascontiguousarray(np.asarray(L0_attn_Wg, np.float32)),
        "Wout": np.ascontiguousarray(np.asarray(L0_attn_Wout, np.float32)),
        "gate": np.ascontiguousarray(np.asarray(L0_ffn_gate_kernel, np.float32)),
        "w0": np.ascontiguousarray(np.asarray(L0_ffn_w0_kernel, np.float32)),
        "w1": np.ascontiguousarray(np.asarray(L0_ffn_w1_kernel, np.float32)),
        "wo": np.ascontiguousarray(np.asarray(L0_ffn_wo_kernel, np.float32)),
    }
    shared.update(_consts())
    nc = build_program()
    in_maps = [dict(shared, x=np.ascontiguousarray(x[i])) for i in range(8)]
    res = run_bass_kernel_spmd(nc, in_maps, core_ids=list(range(8)))
    return np.stack([np.asarray(r["y"], np.float32) for r in res.results], axis=0)
